# version 06
# baseline (speedup 1.0000x reference)
import contextlib
import numpy as np
import ml_dtypes
import concourse.bass as bass
import concourse.mybir as mybir
from concourse.bass_utils import run_bass_kernel_spmd

F32 = mybir.dt.float32
BF16 = mybir.dt.bfloat16
FP8 = mybir.dt.float8e4
AF = mybir.ActivationFunctionType
ALU = mybir.AluOpType

D = 2048
T = 1024
TH = 512
NKC = 16
NH = 16
DI = 6144
NIC = 48
GSZ = 8
NG = NIC // GSZ

ENGINES = ("sync", "tensor", "vector", "scalar", "gpsimd")


class Prog:
    def __init__(self):
        self.streams = {e: [] for e in ENGINES}
        self.count = {e: 0 for e in ENGINES}
        self.cc_count = 0

    def op(self, eng, fn, inc=0):
        self.count[eng] += inc
        self.streams[eng].append(("op", fn, inc))
        return (eng, self.count[eng])

    def dma(self, eng, out, in_):
        return self.op(eng, lambda E: E.dma_start(out=out, in_=in_), inc=16)

    def wait(self, eng, token):
        p, v = token
        if p == "cc":
            self.streams[eng].append(("waitcc", v))
            return
        assert v <= self.count[p], f"wait on future {p}={v}>{self.count[p]}"
        self.streams[eng].append(("wait", p, v))

    def collective(self, fn):
        self.cc_count += 1
        self.streams["gpsimd"].append(("cc", fn))
        return ("cc", self.cc_count)

    def token(self, eng):
        return (eng, self.count[eng])

    def replay(self, name, E, sems, cc_sem):
        for item in self.streams[name]:
            k = item[0]
            if k == "op":
                inst = item[1](E)
                if item[2]:
                    inst.then_inc(sems[name], item[2])
            elif k == "wait":
                E.wait_ge(sems[item[1]], item[2])
            elif k == "waitcc":
                E.wait_ge(cc_sem, item[1])
            else:
                item[1](E).then_inc(cc_sem, 1)


def host_consts():
    n = np.arange(128)
    ang = 2.0 * np.pi * np.outer(n, n) / 128.0
    consts = dict(
        cm=np.cos(ang).astype(ml_dtypes.bfloat16),
        sm=(-np.sin(ang)).astype(ml_dtypes.bfloat16),
        sel16=np.zeros((128, 256), dtype=ml_dtypes.bfloat16),
        selbc=np.zeros((16, 2048), dtype=ml_dtypes.bfloat16),
        negc16=np.full((16, 128), -1.0 / 2048.0, dtype=ml_dtypes.bfloat16),
        posc16=np.full((16, 1), 1.0 / 2048.0, dtype=ml_dtypes.bfloat16),
        ones1x128=np.ones((1, 128), dtype=ml_dtypes.bfloat16),
        ones_sc=np.ones((128, T), dtype=ml_dtypes.bfloat16),
    )
    for h in range(16):
        consts["sel16"][:, 16 * h + h] = 1.0
        consts["selbc"][h, 128 * h:128 * (h + 1)] = 1.0
    return consts


def build_program(taps=()):
    nc = bass.Bass("TRN2", target_bir_lowering=False)
    taps = set(taps)
    tap_list = []

    dx = nc.dram_tensor("x", [D, T], F32, kind="ExternalInput").ap()
    dwq = nc.dram_tensor("wq_t", [D, D], F32, kind="ExternalInput").ap()
    dwkv = nc.dram_tensor("wkv_t", [D, D], F32, kind="ExternalInput").ap()
    dwo = nc.dram_tensor("wo_t", [D, D], F32, kind="ExternalInput").ap()
    dgw = nc.dram_tensor("gw", [D, DI], F32, kind="ExternalInput").ap()
    duw = nc.dram_tensor("uw", [D, DI], F32, kind="ExternalInput").ap()
    ddw = nc.dram_tensor("dw", [DI, D], F32, kind="ExternalInput").ap()
    dgs = nc.dram_tensor("gscol", [128, NIC], F32, kind="ExternalInput").ap()
    dus = nc.dram_tensor("uscol", [128, NIC], F32, kind="ExternalInput").ap()
    dds = nc.dram_tensor("dscol", [128, 16], F32, kind="ExternalInput").ap()
    dmask = nc.dram_tensor("maskag", [128, 8], F32, kind="ExternalInput").ap()
    dcm = nc.dram_tensor("cm", [128, 128], BF16, kind="ExternalInput").ap()
    dsm = nc.dram_tensor("sm", [128, 128], BF16, kind="ExternalInput").ap()
    dsel = nc.dram_tensor("sel16", [128, 256], BF16, kind="ExternalInput").ap()
    dselbc = nc.dram_tensor("selbc", [16, 2048], BF16, kind="ExternalInput").ap()
    dnegc = nc.dram_tensor("negc16", [16, 128], BF16, kind="ExternalInput").ap()
    dposc = nc.dram_tensor("posc16", [16, 1], BF16, kind="ExternalInput").ap()
    dones1 = nc.dram_tensor("ones1x128", [1, 128], BF16, kind="ExternalInput").ap()
    dones_sc = nc.dram_tensor("ones_sc", [128, T], BF16, kind="ExternalInput").ap()
    dout = nc.dram_tensor("out", [D, T], F32, kind="ExternalOutput").ap()
    cc_in = nc.dram_tensor("cc_in", [128, 32], F32)
    cc_out = nc.dram_tensor("cc_out", [128 * 8, 32], F32)

    ctx = contextlib.ExitStack()
    sbuf = lambda nm, sh, dt: ctx.enter_context(nc.sbuf_tensor(nm, sh, dt))
    psum = lambda nm, sh: ctx.enter_context(nc.psum_tensor(nm, sh, F32))

    x_sb = sbuf("x_sb", [128, NKC * T], F32)
    z_sb = sbuf("z_sb", [128, 32 * T], BF16)
    pool = sbuf("pool", [128, 32 * T], BF16)
    cm_sb = sbuf("cm_sb", [128, 128], BF16)
    sm_sb = sbuf("sm_sb", [128, 128], BF16)
    sel_sb = sbuf("sel_sb", [128, 256], BF16)
    selbc_sb = sbuf("selbc_sb", [16, 2048], BF16)
    negc_sb = sbuf("negc_sb", [16, 128], BF16)
    posc_sb = sbuf("posc_sb", [16, 1], BF16)
    ones1_sb = sbuf("ones1_sb", [1, 128], BF16)
    mask_sb = sbuf("mask_sb", [128, 8], F32)
    gs_sb = sbuf("gs_sb", [128, NIC], F32)
    us_sb = sbuf("us_sb", [128, NIC], F32)
    ds_sb = sbuf("ds_sb", [128, 16], F32)
    rnk16 = sbuf("rnk16", [16, T], BF16)
    xb_t = sbuf("xb_t", [128, T], BF16)
    sq_t = sbuf("sq_t", [128, T], BF16)
    wmat = sbuf("wmat", [128, 16], BF16)
    bc2a = sbuf("bc2a", [128, T], BF16)
    sums_sb = sbuf("sums_sb", [128, 32], F32)
    carry_sb = sbuf("carry_sb", [128, 32], F32)
    nrows = z_sb[0:16, 8 * T:9 * T]
    nrows2 = z_sb[0:16, 9 * T:10 * T]
    rnq16 = pool[0:16, 12 * T:13 * T]
    rowbuf = pool[0:1, 16 * T:16 * T + 2 * T]
    poolf32 = pool.bitcast(F32)
    gath_sb = poolf32[:, 7680:7680 + 256]
    ones_sc_sb = pool[:, 11 * T:12 * T]

    ps_a = psum("ps_a", [128, T])
    ps_b = psum("ps_b", [128, T])
    ps_c = psum("ps_c", [128, T])
    ps_d = psum("ps_d", [128, T])
    PS = [ps_a, ps_b]

    P = Prog()

    def slot(i, half=None):
        if half is None:
            return pool[:, i * T:(i + 1) * T]
        return pool[:, i * T + half * TH: i * T + (half + 1) * TH]

    def zslot(i, half=None):
        if half is None:
            return z_sb[:, i * T:(i + 1) * T]
        return z_sb[:, i * T + half * TH: i * T + (half + 1) * TH]

    def xch(fc, half=None):
        if half is None:
            return x_sb[:, fc * T:(fc + 1) * T]
        return x_sb[:, fc * T + half * TH: fc * T + (half + 1) * TH]

    def tap(name, ap, shape, dt, deps):
        if name not in taps:
            return
        t = nc.dram_tensor("tap_" + name, shape, dt, kind="ExternalOutput").ap()
        tap_list.append(name)
        for d in deps:
            P.wait("sync", d)
        P.dma("sync", t, ap)

    ld_x = P.dma("sync", x_sb.rearrange("p (fc t) -> p fc t", t=T),
                 dx.rearrange("(fc p) t -> p fc t", p=128))
    for dst, src in ((cm_sb, dcm), (sm_sb, dsm), (sel_sb, dsel), (selbc_sb, dselbc),
                     (negc_sb, dnegc), (posc_sb, dposc), (ones1_sb, dones1),
                     (mask_sb, dmask),
                     (gs_sb, dgs), (us_sb, dus), (ds_sb, dds)):
        P.dma("sync", dst[:, :], src[:, :])
    ld_consts = P.token("sync")

    def wtile(buf, kc, hq):
        base = (16 + 8 * buf) * T + kc * TH + hq * 128
        return pool[:, base:base + 128]

    def stream_quarter(dw_, qn, buf, guard=None):
        if guard is not None:
            P.wait("gpsimd", guard)
        base = (16 + 8 * buf) * T
        dst = pool[:, base:base + 8 * T].rearrange("p (kc c) -> p kc c", c=TH)
        src = dw_.rearrange("(kc p) d -> p kc d", p=128)[:, :, TH * qn:TH * (qn + 1)]
        return P.dma("gpsimd", dst, src)

    P.wait("scalar", ld_x)
    P.wait("tensor", ld_consts)
    mm_prev = None
    for i in range(2 * NKC):
        fc, th = divmod(i, 2)
        cb = xb_t[:, (i % 2) * TH:((i % 2) + 1) * TH]
        if i >= 2:
            P.wait("scalar", mm_prev)
        ctk = P.op("scalar", lambda E, o=cb, s=xch(fc, th): E.copy(out=o, in_=s), inc=1)
        P.wait("tensor", ctk)
        mm_prev = P.op("tensor", lambda E, o=ps_c[0:16, th * TH:(th + 1) * TH],
                       l=sel_sb[:, 16 * fc:16 * fc + 16], r=cb,
                       s=(fc == 0), st=(fc == NKC - 1):
                       E.matmul(o, l, r, start=s, stop=st), inc=1)
    P.wait("vector", mm_prev)
    ev1 = P.op("vector", lambda E: E.tensor_copy(nrows, ps_c[0:16, :]), inc=1)
    P.wait("tensor", ev1)
    bcmm = None
    for th in range(2):
        bcmm = P.op("tensor", lambda E, o=ps_a[:, th * TH:(th + 1) * TH],
                    r=nrows[:, th * TH:(th + 1) * TH]:
                    E.matmul(o, negc_sb[:, :], r, start=True, stop=True), inc=(th == 1))
    bneg1 = zslot(31)
    P.wait("scalar", bcmm)
    bneg1_tok = P.op("scalar", lambda E: E.copy(out=bneg1, in_=ps_a[:, :]), inc=1)
    x1_tok = None
    x1_chunk = {}
    for fc in range(NKC):
        P.wait("vector", bneg1_tok)
        x1_chunk[fc] = P.op("vector", lambda E, o=slot(fc), a=xch(fc):
                            E.tensor_add(o, a, bneg1), inc=1)
        x1_tok = x1_chunk[fc]
    tap("x1", pool[:, 0:NKC * T], [128, NKC * T], BF16, [x1_tok])

    sqbufs = [xb_t, sq_t]

    def proj_head(h, wbuf, psn, psz, first=False, nrow=0):
        hq = h % 4
        for th in range(2):
            for kc in range(NKC):
                if first and th == 0:
                    P.wait("tensor", x1_chunk[kc])
                last = kc == NKC - 1
                P.op("tensor", lambda E, o=psz[:, th * TH:(th + 1) * TH],
                     l=wtile(wbuf, kc, hq), r=slot(kc, th), s=(kc == 0), st=last:
                     E.matmul(o, l, r, start=s, stop=st), inc=1 if last else 0)
        mm_tok = P.token("tensor")
        P.wait("scalar", mm_tok)
        zdst = zslot(h + (0 if nrow == 32 else 16))
        zev = P.op("scalar", lambda E, o=zdst, i=psz[:, :]: E.copy(out=o, in_=i), inc=1)
        sqb = sqbufs[h % 2]
        sqev = P.op("scalar", lambda E, o=sqb[:, :], i=psz[:, :]:
                    E.activation(out=o, in_=i, func=AF.Square), inc=1)
        P.wait("tensor", sqev)
        nmm = None
        for th in range(2):
            nmm = P.op("tensor", lambda E, o=psn[nrow:nrow + 16, th * TH:(th + 1) * TH],
                       l=sel_sb[:, 16 * h:16 * h + 16], r=sqb[:, th * TH:(th + 1) * TH],
                       s=(h == 0), st=(h == NH - 1):
                       E.matmul(o, l, r, start=s, stop=st),
                       inc=(1 if (h == NH - 1 and th == 1) else 0))
        return mm_tok, zev, sqev, nmm

    zf32 = z_sb.bitcast(F32)
    def finalize_norm_q(psn, scratch, dst):
        tok = P.op("vector", lambda E: E.reciprocal(scratch, psn[32:48, :]), inc=1)
        P.wait("scalar", tok)
        return P.op("scalar", lambda E: E.activation(out=dst, in_=scratch,
                                                     func=AF.Sqrt, scale=1.0 / 128.0), inc=1)

    def finalize_norm(psn, scratch, dst):
        tok = P.op("vector", lambda E: E.reciprocal(scratch, psn[0:16, :]), inc=1)
        P.wait("scalar", tok)
        return P.op("scalar", lambda E: E.activation(out=dst, in_=scratch,
                                                     func=AF.Sqrt, scale=1.0 / 128.0), inc=1)

    qtr_done = {}
    pref = {("kv", 0): stream_quarter(dwkv, 0, 0), ("kv", 1): stream_quarter(dwkv, 1, 1)}
    guards = {}
    kv_nmm = None
    for h in range(NH):
        qn, buf = h // 4, (h // 4) % 2
        if h % 4 == 0:
            P.wait("tensor", pref[("kv", qn)])
        if h >= 3:
            P.wait("tensor", guards[h - 3])
        mm_tok, zev, sqev, kv_nmm = proj_head(h, buf, ps_d, [ps_a, ps_b, ps_c][h % 3], first=(h == 0))
        guards[h] = sqev
        if h % 4 == 3:
            qtr_done[("kv", qn)] = mm_tok
            if qn < 3:
                pref[("kv", qn + 1)] = pref.get(("kv", qn + 1)) or \
                    stream_quarter(dwkv, qn + 1, (qn + 1) % 2,
                                   guard=qtr_done.get(("kv", qn - 1)))
    P.wait("vector", kv_nmm)
    rnk_tok = finalize_norm(ps_d, zf32[0:16, 0:T], rnk16[:, :])
    tap("rnk16", rnk16[:, :], [16, T], BF16, [rnk_tok])

    pref[("q", 0)] = stream_quarter(dwq, 0, 0, guard=qtr_done[("kv", 2)])
    pref[("q", 1)] = stream_quarter(dwq, 1, 1, guard=qtr_done[("kv", 3)])
    P.wait("tensor", rnk_tok)
    P.wait("tensor", kv_nmm)
    prev_bcev = None
    cst = None
    sums_ev = None
    cc_send = None
    cc_tok = None
    smm = None
    q_nmm = None
    def sums_head(hs):
        bcmm = None
        for th in range(2):
            bcmm = P.op("tensor", lambda E, o=ps_c[:, th * TH:(th + 1) * TH],
                        l=selbc_sb[:, 128 * hs:128 * (hs + 1)],
                        r=rnk16[:, th * TH:(th + 1) * TH]:
                        E.matmul(o, l, r, start=True, stop=True), inc=(th == 1))
        P.wait("scalar", bcmm)
        bcev = P.op("scalar", lambda E: E.copy(out=bc2a[:, :], in_=ps_c[:, :]), inc=1)
        P.wait("vector", bcev)
        cst = P.op("vector", lambda E, z=zslot(16 + hs), hh=hs: E.scalar_tensor_tensor(
            bc2a[:, :], z, 1.0, bc2a[:, :], ALU.mult, ALU.mult,
            accum_out=wmat[:, hh:hh + 1]), inc=1)
        return bcev, cst

    for h in range(NH):
        qn, buf = h // 4, (h // 4) % 2
        if h < 8:
            for hs in (2 * h, 2 * h + 1):
                if prev_bcev is not None:
                    P.wait("tensor", prev_bcev)
                prev_bcev, cst = sums_head(hs)
        if h == 8:
            tap("wmat", wmat[:, :], [128, 16], BF16, [cst])
            P.wait("tensor", cst)
            P.op("tensor", lambda E: E.matmul(ps_c[:, 0:16], cm_sb[:, :], wmat[:, :],
                                              start=True, stop=True), inc=0)
            smm = P.op("tensor", lambda E: E.matmul(ps_c[:, 16:32], sm_sb[:, :], wmat[:, :],
                                                    start=True, stop=True), inc=1)
            P.wait("vector", smm)
            sums_ev = P.op("vector", lambda E: E.tensor_copy(sums_sb[:, :], ps_c[:, 0:32]), inc=1)
            P.wait("sync", sums_ev)
            cc_send = P.dma("sync", cc_in.ap()[:, :], sums_sb[:, :])
            P.wait("gpsimd", cc_send)
            cc_tok = P.collective(lambda E: E.collective_compute(
                "AllGather", ALU.bypass, replica_groups=[list(range(8))],
                ins=[cc_in.ap().opt()], outs=[cc_out.ap().opt()]))
        if h == 0:
            P.wait("tensor", rnk_tok)
        if h % 4 == 0:
            P.wait("tensor", pref[("q", qn)])
        if h == 0:
            P.wait("tensor", guards[15])
        elif h == 1:
            P.wait("tensor", guards[13])
        else:
            P.wait("tensor", guards[NH + h - 2])
        mm_tok, zev, sqev, q_nmm = proj_head(h, buf, ps_d, PS[h % 2], nrow=32)
        guards[NH + h] = sqev
        if h % 4 == 3:
            qtr_done[("q", qn)] = mm_tok
            if qn < 3:
                pref[("q", qn + 1)] = pref.get(("q", qn + 1)) or \
                    stream_quarter(dwq, qn + 1, (qn + 1) % 2,
                                   guard=qtr_done.get(("q", qn - 1)))
    P.wait("vector", q_nmm)
    rnq_tok = finalize_norm_q(ps_d, poolf32[0:16, 6656:6656 + T], rnq16)
    tap("rnq16", rnq16, [16, T], BF16, [rnq_tok])
    tap("zq0", zslot(0), [128, T], BF16, [guards[NH]])
    tap("zkv0", zslot(16), [128, T], BF16, [guards[0]])
    tap("sums", sums_sb[:, :], [128, 32], F32, [sums_ev])
    P.wait("sync", qtr_done[("q", 3)])
    ones_ld = P.dma("sync", ones_sc_sb, dones_sc[:, :])

    P.wait("sync", cc_tok)
    P.wait("sync", qtr_done[("q", 3)])
    gl = None
    for r in range(8):
        gl = P.dma("sync", gath_sb[:, 32 * r:32 * (r + 1)],
                   cc_out.ap()[128 * r:128 * (r + 1), :])
    P.wait("vector", gl)
    P.op("vector", lambda E: E.memset(carry_sb[:, :], 0.0), inc=0)
    carry_tok = None
    for r in range(8):
        carry_tok = P.op("vector", lambda E, rr=r: E.scalar_tensor_tensor(
            carry_sb[:, :], gath_sb[:, 32 * rr:32 * (rr + 1)], mask_sb[:, rr:rr + 1],
            carry_sb[:, :], ALU.mult, ALU.add), inc=1)
    tap("carry", carry_sb[:, :], [128, 32], F32, [carry_tok])

    t_cr, t_ci = slot(2), slot(3)
    t_qr, t_qi = slot(4), slot(5)
    t_vr, t_vi = slot(6), slot(7)
    t_sc, t_sc2 = slot(8), slot(9)
    zs2 = [slot(13), slot(10)]
    wo_tmp = poolf32[:, 7168:7168 + T]
    pref[("o", 0)] = stream_quarter(dwo, 0, 0, guard=qtr_done[("q", 2)])
    pref[("o", 1)] = stream_quarter(dwo, 1, 1, guard=qtr_done[("q", 3)])
    flush1 = {}
    prev_tkv = None
    prev_tq = None
    prev_e2 = None
    prev_vev = None
    prev_q1 = None
    prev_s2 = None
    wo1_dc = 0
    wo1_flush = None
    wo1_mm = None

    def wo_half1_step(vev_tok, q2_tok):
        nonlocal wo1_dc, wo1_flush, wo1_mm
        if wo1_dc >= 16:
            return
        dc = wo1_dc
        qn, dcq = dc // 4, dc % 4
        P.wait("tensor", pref[("o", qn)])
        if wo1_flush is not None:
            P.wait("tensor", wo1_flush)
        else:
            P.wait("tensor", q2_tok)
        mmv = None
        for th in range(2):
            for hh in range(8):
                last = hh == 7
                mmv = P.op("tensor", lambda E, o=ps_d[:, th * TH:(th + 1) * TH],
                           l=wtile(qn % 2, hh, dcq), r=zslot(16 + hh, th),
                           s=(hh == 0), st=last:
                           E.matmul(o, l, r, start=s, stop=st),
                           inc=(1 if (last and th == 1) else 0))
        wo1_mm = mmv
        if dc % 2 == 0:
            P.wait("vector", mmv)
            wo1_flush = P.op("vector", lambda E, o=xch(dc), i=ps_d[:, :]:
                             E.tensor_add(o, o, i), inc=1)
        else:
            P.wait("scalar", mmv)
            ev = P.op("scalar", lambda E: E.copy(out=wo_tmp, in_=ps_d[:, :]), inc=1)
            P.wait("gpsimd", ev)
            wo1_flush = P.op("gpsimd", lambda E, o=xch(dc):
                             E.tensor_tensor(o, o, wo_tmp, ALU.add), inc=1)
        flush1[dc] = wo1_flush
        wo1_dc += 1
        if wo1_dc == 4:
            pref[("o", 2)] = stream_quarter(dwo, 2, 0, guard=wo1_mm)
        if wo1_dc == 8:
            pref[("o", 3)] = stream_quarter(dwo, 3, 1, guard=wo1_mm)
        if wo1_dc == 12:
            pref[("o2", 0)] = stream_quarter(dwo, 0, 0, guard=wo1_mm)
        if wo1_dc == 16:
            pref[("o2", 1)] = stream_quarter(dwo, 1, 1, guard=wo1_mm)

    for h in range(NH):
        P.wait("tensor", rnq_tok)
        P.wait("tensor", carry_tok)
        zs = zs2[h % 2]
        if h == 0:
            P.wait("tensor", qtr_done[("q", 3)])
        if prev_vev is not None:
            P.wait("tensor", prev_vev)
        mm1 = mm2 = None
        for th in range(2):
            mm1 = P.op("tensor", lambda E, o=ps_a[:, th * TH:(th + 1) * TH],
                       l=selbc_sb[:, 128 * h:128 * (h + 1)],
                       r=rnk16[:, th * TH:(th + 1) * TH]:
                       E.matmul(o, l, r, start=True, stop=True), inc=(th == 1))
        if prev_e2 is not None:
            P.wait("tensor", prev_e2)
        for th in range(2):
            mm2 = P.op("tensor", lambda E, o=ps_b[:, th * TH:(th + 1) * TH],
                       l=selbc_sb[:, 128 * h:128 * (h + 1)],
                       r=rnq16[:, th * TH:(th + 1) * TH]:
                       E.matmul(o, l, r, start=True, stop=True), inc=(th == 1))
        P.wait("scalar", mm1)
        e1 = P.op("scalar", lambda E: E.copy(out=slot(0), in_=ps_a[:, :]), inc=1)
        P.wait("gpsimd", e1)
        tkv = P.op("gpsimd", lambda E, z=zslot(16 + h), o=zs:
                   E.tensor_tensor(o, z, slot(0), ALU.mult), inc=1)
        P.wait("tensor", tkv)
        if prev_q1 is not None:
            P.wait("tensor", prev_q1)
        f1 = f2 = None
        for th in range(2):
            f1 = P.op("tensor", lambda E, o=ps_c[:, th * TH:(th + 1) * TH], rr=(th * TH, (th + 1) * TH), z=zs:
                      E.matmul(o, cm_sb[:, :], z[:, rr[0]:rr[1]], start=True, stop=True), inc=(th == 1))
        if h >= 8:
            if wo1_flush is not None:
                P.wait("tensor", wo1_flush)
        for th in range(2):
            f2 = P.op("tensor", lambda E, o=ps_d[:, th * TH:(th + 1) * TH], rr=(th * TH, (th + 1) * TH), z=zs:
                      E.matmul(o, sm_sb[:, :], z[:, rr[0]:rr[1]], start=True, stop=True), inc=(th == 1))
        if h == 0:
            P.wait("vector", ones_ld)
        P.wait("vector", f1)
        s1 = P.op("vector", lambda E, ic=carry_sb[:, h:h + 1]:
                  E.tensor_tensor_scan(t_cr, ones_sc_sb, ps_c[:, :], ic,
                                       ALU.mult, ALU.add), inc=1)
        P.wait("vector", f2)
        s2 = P.op("vector", lambda E, ic=carry_sb[:, 16 + h:16 + h + 1]:
                  E.tensor_tensor_scan(t_ci, ones_sc_sb, ps_d[:, :], ic,
                                       ALU.mult, ALU.add), inc=1)
        zsq = zs2[(h + 1) % 2]
        P.wait("scalar", mm2)
        e2 = P.op("scalar", lambda E: E.copy(out=slot(1), in_=ps_b[:, :]), inc=1)
        P.wait("vector", e2)
        tq = P.op("vector", lambda E, z=zslot(h), o=zsq:
                  E.tensor_mul(o, z, slot(1)), inc=1)
        P.wait("tensor", tq)
        P.wait("tensor", s1)
        g1 = g2 = None
        for th in range(2):
            g1 = P.op("tensor", lambda E, o=ps_c[:, th * TH:(th + 1) * TH], rr=(th * TH, (th + 1) * TH), z=zsq:
                      E.matmul(o, cm_sb[:, :], z[:, rr[0]:rr[1]], start=True, stop=True), inc=(th == 1))
        P.wait("tensor", s2)
        for th in range(2):
            g2 = P.op("tensor", lambda E, o=ps_d[:, th * TH:(th + 1) * TH], rr=(th * TH, (th + 1) * TH), z=zsq:
                      E.matmul(o, sm_sb[:, :], z[:, rr[0]:rr[1]], start=True, stop=True), inc=(th == 1))
        P.wait("scalar", g1)
        q1 = P.op("scalar", lambda E: E.copy(out=t_qr, in_=ps_c[:, :]), inc=1)
        P.wait("scalar", g2)
        q2 = P.op("scalar", lambda E: E.copy(out=t_qi, in_=ps_d[:, :]), inc=1)
        P.wait("gpsimd", q2)
        P.wait("gpsimd", s2)
        gp1 = P.op("gpsimd", lambda E: E.tensor_tensor(t_sc, t_qi, t_ci, ALU.mult), inc=1)
        P.wait("vector", q1)
        P.op("vector", lambda E: E.tensor_mul(t_vr, t_qr, t_cr), inc=0)
        P.wait("vector", q2)
        P.op("vector", lambda E: E.tensor_mul(t_vi, t_qi, t_cr), inc=0)
        P.wait("vector", gp1)
        P.op("vector", lambda E: E.tensor_sub(t_vr, t_vr, t_sc), inc=0)
        P.op("vector", lambda E: E.tensor_mul(t_sc2, t_qr, t_ci), inc=0)
        prod = P.op("vector", lambda E: E.tensor_add(t_vi, t_sc2, t_vi), inc=1)
        P.wait("tensor", prod)
        vm = None
        for th in range(2):
            P.op("tensor", lambda E, o=ps_a[:, th * TH:(th + 1) * TH], rr=(th * TH, (th + 1) * TH):
                 E.matmul(o, cm_sb[:, :], t_vr[:, rr[0]:rr[1]], start=True, stop=False), inc=0)
            vm = P.op("tensor", lambda E, o=ps_a[:, th * TH:(th + 1) * TH], rr=(th * TH, (th + 1) * TH):
                      E.matmul(o, sm_sb[:, :], t_vi[:, rr[0]:rr[1]], start=False, stop=True), inc=(th == 1))
        P.wait("scalar", vm)
        vev = P.op("scalar", lambda E, o=zslot(16 + h): E.activation(
            out=o, in_=ps_a[:, :], func=AF.Copy, scale=1.0 / 128.0), inc=1)
        prev_tkv, prev_tq, prev_vev, prev_q1, prev_s2, prev_e2 = tkv, tq, vev, q1, s2, e2
        if h >= 8:
            wo_half1_step(vev, q2)
            wo_half1_step(vev, q2)
        if h == 0:
            tap("kvc0", t_cr, [128, T], BF16, [s1])
            tap("qfn0", t_qr, [128, T], BF16, [q1])
            tap("v0", zslot(16), [128, T], BF16, [vev])

    flush = {}
    nfl = 0
    last_flush = None
    mm_m = mm_s = None
    ln_prev = None
    h2mm = {}
    for qn in range(4):
        buf = qn % 2
        if qn == 2:
            pref[("o2", 2)] = stream_quarter(dwo, 2, 0, guard=h2mm[0])
        if qn == 3:
            pref[("o2", 3)] = stream_quarter(dwo, 3, 1, guard=h2mm[1])
        P.wait("tensor", pref[("o2", qn)])
        for dcq in range(4):
            dc = 4 * qn + dcq
            psd = PS[dc % 2]
            if nfl >= 2:
                P.wait("tensor", flush[nfl - 2])
            elif nfl == 0:
                P.wait("tensor", prev_vev)
            mmv = None
            for th in range(2):
                for hh in range(8, NH):
                    last = hh == NH - 1
                    mmv = P.op("tensor", lambda E, o=psd[:, th * TH:(th + 1) * TH],
                               l=wtile(buf, hh, dcq), r=zslot(16 + hh, th),
                               s=(hh == 8), st=last:
                               E.matmul(o, l, r, start=s, stop=st),
                               inc=(1 if (last and th == 1) else 0))
            P.wait("vector", mmv)
            P.wait("vector", flush1[dc])
            ft = P.op("vector", lambda E, o=xch(dc), i=psd[:, :]:
                      E.tensor_add(o, o, i), inc=1)
            flush[nfl] = ft
            nfl += 1
            last_flush = ft
            h2mm[qn] = mmv
            cb = [xb_t, sq_t][dc % 2]
            P.wait("scalar", ft)
            if ln_prev is not None:
                P.wait("scalar", ln_prev)
            c1 = P.op("scalar", lambda E, o=cb[:, :], s=xch(dc): E.copy(out=o, in_=s), inc=1)
            c2 = P.op("scalar", lambda E, o=bc2a[:, :], s=xch(dc):
                      E.activation(out=o, in_=s, func=AF.Square), inc=1)
            P.wait("tensor", c2)
            for th in range(2):
                mm_m = P.op("tensor", lambda E, o=ps_c[0:16, th * TH:(th + 1) * TH],
                            l=sel_sb[:, 16 * dc:16 * dc + 16], r=cb[:, th * TH:(th + 1) * TH],
                            s=(dc == 0), st=(dc == NKC - 1):
                            E.matmul(o, l, r, start=s, stop=st), inc=0)
                mm_s = P.op("tensor", lambda E, o=ps_d[0:16, th * TH:(th + 1) * TH],
                            l=sel_sb[:, 16 * dc:16 * dc + 16], r=bc2a[:, th * TH:(th + 1) * TH],
                            s=(dc == 0), st=(dc == NKC - 1):
                            E.matmul(o, l, r, start=s, stop=st), inc=(1 if th == 1 else 0))
            ln_prev = mm_s
    tap("xattn", x_sb[:, :], [128, NKC * T], F32, [last_flush])

    P.wait("vector", mm_s)
    P.op("vector", lambda E: E.tensor_copy(nrows, ps_c[0:16, :]), inc=0)
    e_s = P.op("vector", lambda E: E.tensor_copy(nrows2, ps_d[0:16, :]), inc=1)
    P.wait("tensor", e_s)
    r_m = r_s = None
    for th in range(2):
        r_m = P.op("tensor", lambda E, o=ps_a[0:1, th * TH:(th + 1) * TH],
                   r=nrows[:, th * TH:(th + 1) * TH]:
                   E.matmul(o, posc_sb[:, :], r, start=True, stop=True), inc=(th == 1))
        r_s = P.op("tensor", lambda E, o=ps_b[0:1, th * TH:(th + 1) * TH],
                   r=nrows2[:, th * TH:(th + 1) * TH]:
                   E.matmul(o, posc_sb[:, :], r, start=True, stop=True), inc=(th == 1))
    P.wait("vector", r_s)
    rowm = poolf32[0:1, 9216:9216 + T]
    rowv = poolf32[0:1, 10240:10240 + T]
    rowr = poolf32[0:1, 11264:11264 + T]
    P.op("vector", lambda E: E.tensor_copy(rowm, ps_a[0:1, :]), inc=0)
    P.op("vector", lambda E: E.tensor_mul(rowv, ps_a[0:1, :], rowm), inc=0)
    P.op("vector", lambda E: E.tensor_sub(rowv, ps_b[0:1, :], rowv), inc=0)
    P.op("vector", lambda E: E.tensor_scalar_add(rowv, rowv, 1e-5), inc=0)
    rec = P.op("vector", lambda E: E.reciprocal(rowr, rowv), inc=1)
    nm2 = P.op("vector", lambda E: E.tensor_scalar_mul(rowbuf[0:1, 0:T], rowm, -1.0), inc=1)
    P.wait("scalar", rec)
    r2t = P.op("scalar", lambda E: E.activation(out=rowbuf[0:1, T:2 * T],
                                                in_=rowr, func=AF.Sqrt), inc=1)
    P.wait("tensor", r2t)
    P.wait("tensor", nm2)
    b_m = b_r = None
    for th in range(2):
        b_m = P.op("tensor", lambda E, o=ps_c[:, th * TH:(th + 1) * TH],
                   r=rowbuf[0:1, th * TH:(th + 1) * TH]:
                   E.matmul(o, ones1_sb[:, :], r, start=True, stop=True), inc=(th == 1))
        b_r = P.op("tensor", lambda E, o=ps_d[:, th * TH:(th + 1) * TH],
                   r=rowbuf[0:1, T + th * TH:T + (th + 1) * TH]:
                   E.matmul(o, ones1_sb[:, :], r, start=True, stop=True), inc=(th == 1))
    bneg2, br2 = zslot(31), zslot(30)
    P.wait("scalar", b_m)
    eb1x = P.op("scalar", lambda E: E.copy(out=bneg2, in_=ps_c[:, :]), inc=1)
    eb2 = P.op("scalar", lambda E: E.copy(out=br2, in_=ps_d[:, :]), inc=1)
    x2_tok = None
    x2_chunk = {}
    for fc in range(NKC):
        P.wait("vector", eb1x)
        P.op("vector", lambda E, o=slot(fc), a=xch(fc): E.tensor_add(o, a, bneg2), inc=0)
    for fc in range(NKC):
        P.wait("vector", eb2)
        x2_chunk[fc] = P.op("vector", lambda E, o=slot(fc): E.tensor_mul(o, o, br2), inc=1)
        x2_tok = x2_chunk[fc]
    tap("x2", pool[:, 0:NKC * T], [128, NKC * T], BF16, [x2_tok])

    z8 = z_sb.bitcast(FP8)

    def dgrp(i):
        return z_sb[:, 2048 * i:2048 * (i + 1)]

    def gbuf(b):
        return z_sb[:, 16384 + 2048 * b:16384 + 2048 * (b + 1)]

    def ubuf(b):
        return z_sb[:, 20480 + 2048 * b:20480 + 2048 * (b + 1)]

    def g8(b):
        return z8[:, 49152 + 2048 * b:49152 + 2048 * (b + 1)]

    def u8(b):
        return z8[:, 53248 + 2048 * b:53248 + 2048 * (b + 1)]

    def d8(b):
        return z8[:, 57344 + 2048 * b:57344 + 2048 * (b + 1)]

    gsrc = dgw.rearrange("(kc p) i -> p kc i", p=128)
    usrc = duw.rearrange("(kc p) i -> p kc i", p=128)

    def load_gu(ic, b, guard):
        if guard is not None:
            P.wait("gpsimd", guard)
        P.dma("gpsimd", g8(b).rearrange("p (kc c) -> p kc c", c=128),
              gsrc[:, :, 128 * ic:128 * (ic + 1)])
        t = P.dma("gpsimd", u8(b).rearrange("p (kc c) -> p kc c", c=128),
                  usrc[:, :, 128 * ic:128 * (ic + 1)])
        t2 = P.dma("gpsimd", d8(b), ddw[128 * ic:128 * (ic + 1), :])
        return t2

    ffn_ld = {0: load_gu(0, 0, (P.token("tensor"))), 1: load_gu(1, 1, None)}
    upc = {}
    fuse_tok = {}
    dmm_done = {}
    flf = {}
    silu_tok = {}
    nffl = 0
    last_ffn = None
    st_out = None
    GPS = [ps_a, ps_c]
    UPS = [ps_b, ps_d]
    for g in range(NG):
        for icg in range(GSZ):
            ic = g * GSZ + icg
            b = ic % 2
            P.wait("scalar", ffn_ld[ic])
            if ic >= 2:
                P.wait("scalar", fuse_tok[ic - 2])
            if g > 0:
                P.wait("scalar", dmm_done[g - 1])
            ug = P.op("scalar", lambda E, o=gbuf(b), i=g8(b): E.copy(out=o, in_=i), inc=1)
            P.wait("vector", ffn_ld[ic])
            if ic >= 2:
                P.wait("vector", fuse_tok[ic - 2])
            if g > 0:
                P.wait("vector", dmm_done[g - 1])
            uu = P.op("vector", lambda E, o=ubuf(b), i=u8(b): E.tensor_copy(o, i), inc=1)
            ud = P.op("vector", lambda E, o=dgrp(icg), i=d8(b): E.tensor_copy(o, i), inc=1)
            upc[ic] = ud
            if ic + 2 < NIC:
                P.wait("gpsimd", ug)
                ffn_ld[ic + 2] = load_gu(ic + 2, b, ud)
            P.wait("tensor", ug)
            P.wait("tensor", uu)
            if ic == 0:
                P.wait("tensor", eb2)
            if ic >= 2:
                P.wait("tensor", fuse_tok[ic - 2])
            if ic < 2:
                pass
            if icg == 0 and g > 0:
                P.wait("tensor", flf[nffl - 1])
            mg = mu = None
            psg, psu = GPS[ic % 2], UPS[ic % 2]
            for th in range(2):
                for kc in range(NKC):
                    if ic == 0 and th == 0:
                        P.wait("tensor", x2_chunk[kc])
                    last = kc == NKC - 1
                    mg = P.op("tensor", lambda E, o=psg[:, th * TH:(th + 1) * TH],
                              l=gbuf(b)[:, 128 * kc:128 * (kc + 1)], r=slot(kc, th),
                              s=(kc == 0), st=last:
                              E.matmul(o, l, r, start=s, stop=st),
                              inc=(1 if (last and th == 1) else 0))
            for th in range(2):
                for kc in range(NKC):
                    last = kc == NKC - 1
                    mu = P.op("tensor", lambda E, o=psu[:, th * TH:(th + 1) * TH],
                              l=ubuf(b)[:, 128 * kc:128 * (kc + 1)], r=slot(kc, th),
                              s=(kc == 0), st=last:
                              E.matmul(o, l, r, start=s, stop=st),
                              inc=(1 if (last and th == 1) else 0))
            gact = [xb_t[:, :], zslot(30)][ic % 2]
            uact = [sq_t[:, :], zslot(31)][ic % 2]
            P.wait("scalar", mg)
            sg = P.op("scalar", lambda E, i=ic, p=psg, o=gact: E.activation(
                out=o, in_=p[:, :], func=AF.Silu, scale=gs_sb[:, i:i + 1]), inc=1)
            silu_tok[ic] = sg
            P.wait("scalar", mu)
            su = P.op("scalar", lambda E, i=ic, p=psu, o=uact: E.activation(
                out=o, in_=p[:, :], func=AF.Copy, scale=us_sb[:, i:i + 1]), inc=1)
            P.wait("vector", su)
            fuse_tok[ic] = P.op("vector", lambda E, o=slot(16 + icg), a=gact, u=uact:
                                E.tensor_mul(o, a, u), inc=1)
        dlast = None
        for dc in range(16):
            psd = [ps_c, ps_d][dc % 2]
            if nffl >= 2:
                P.wait("tensor", flf[nffl - 2])
            if dc < 2:
                P.wait("tensor", silu_tok[g * GSZ + GSZ - 1])
                P.wait("tensor", fuse_tok[g * GSZ + GSZ - 1])
                P.wait("tensor", upc[g * GSZ + GSZ - 1])
            mmv = None
            for th in range(2):
                for icg in range(GSZ):
                    last = icg == GSZ - 1
                    mmv = P.op("tensor", lambda E, o=psd[:, th * TH:(th + 1) * TH],
                               l=dgrp(icg)[:, 128 * dc:128 * (dc + 1)],
                               r=slot(16 + icg, th), s=(icg == 0), st=last:
                               E.matmul(o, l, r, start=s, stop=st),
                               inc=(1 if (last and th == 1) else 0))
            dlast = mmv
            P.wait("vector", mmv)
            ft = P.op("vector", lambda E, o=xch(dc), i=psd[:, :], d=dc:
                      E.scalar_tensor_tensor(o, i, ds_sb[:, d:d + 1], o,
                                             ALU.mult, ALU.add), inc=1)
            flf[nffl] = ft
            nffl += 1
            last_ffn = ft
            if g == NG - 1:
                P.wait("sync", ft)
                st_out = P.dma("sync", dout.rearrange("(fc p) t -> p fc t", p=128)[:, dc, :],
                               xch(dc))
        dmm_done[g] = dlast
    P.wait("sync", st_out)

    with contextlib.ExitStack() as es:
        sems = {e: es.enter_context(nc.semaphore("sem_" + e)) for e in ENGINES}
        cc_sem = es.enter_context(nc.semaphore("sem_cc"))
        with nc.Block() as block:
            block.sync(lambda E: P.replay("sync", E, sems, cc_sem))
            block.tensor(lambda E: P.replay("tensor", E, sems, cc_sem))
            block.vector(lambda E: P.replay("vector", E, sems, cc_sem))
            block.scalar(lambda E: P.replay("scalar", E, sems, cc_sem))
            block.gpsimd(lambda E: P.replay("gpsimd", E, sems, cc_sem))
    ctx.close()
    return nc, tap_list


_BUILD_CACHE = {}


def _get_program(taps=()):
    key = tuple(sorted(taps))
    if key not in _BUILD_CACHE:
        _BUILD_CACHE[key] = build_program(taps)
    return _BUILD_CACHE[key]


def make_in_maps(L0_attn_x, L0_attn_W_q, L0_attn_W_kv, L0_attn_W_o,
                 L0_ffn_gate_w, L0_ffn_up_w, L0_ffn_down_w,
                 L0_ffn_gate_scale, L0_ffn_up_scale, L0_ffn_down_scale):
    x = np.asarray(L0_attn_x, dtype=np.float32)
    wq_t = np.ascontiguousarray(np.asarray(L0_attn_W_q, dtype=np.float32).T)
    wkv_t = np.ascontiguousarray(np.asarray(L0_attn_W_kv, dtype=np.float32).T)
    wo_t = np.ascontiguousarray(np.asarray(L0_attn_W_o, dtype=np.float32).T)
    gw = np.ascontiguousarray(np.asarray(L0_ffn_gate_w, dtype=np.float32))
    uw = np.ascontiguousarray(np.asarray(L0_ffn_up_w, dtype=np.float32))
    dw = np.ascontiguousarray(np.asarray(L0_ffn_down_w, dtype=np.float32))
    gscol = np.ascontiguousarray(np.asarray(L0_ffn_gate_scale, np.float32).reshape(NIC, 128).T)
    uscol = np.ascontiguousarray(np.asarray(L0_ffn_up_scale, np.float32).reshape(NIC, 128).T)
    dscol = np.ascontiguousarray(np.asarray(L0_ffn_down_scale, np.float32).reshape(16, 128).T)
    consts = host_consts()
    in_maps = []
    for c in range(8):
        b, hh = divmod(c, 2)
        xs = np.ascontiguousarray(x[b, T * hh:T * (hh + 1), :].T)
        mask = np.zeros((128, 8), dtype=np.float32)
        if hh == 1:
            mask[:, c - 1] = 1.0
        m = dict(x=xs, wq_t=wq_t, wkv_t=wkv_t, wo_t=wo_t, gw=gw, uw=uw, dw=dw,
                 gscol=gscol, uscol=uscol, dscol=dscol, maskag=mask, **consts)
        in_maps.append(m)
    return in_maps


def kernel(L0_attn_x, L0_attn_W_q, L0_attn_W_kv, L0_attn_W_o,
           L0_ffn_gate_w, L0_ffn_up_w, L0_ffn_down_w,
           L0_ffn_gate_scale, L0_ffn_up_scale, L0_ffn_down_scale):
    nc, _ = _get_program()
    in_maps = make_in_maps(L0_attn_x, L0_attn_W_q, L0_attn_W_kv, L0_attn_W_o,
                           L0_ffn_gate_w, L0_ffn_up_w, L0_ffn_down_w,
                           L0_ffn_gate_scale, L0_ffn_up_scale, L0_ffn_down_scale)
    res = run_bass_kernel_spmd(nc, in_maps, core_ids=list(range(8)))
    out = np.empty((4, 2048, 2048), dtype=np.float32)
    for c in range(8):
        b, hh = divmod(c, 2)
        out[b, T * hh:T * (hh + 1), :] = np.asarray(res.results[c]["out"]).T
    return out


# version 07
# speedup vs baseline: 1.1047x; 1.1047x over previous
import contextlib
import numpy as np
import ml_dtypes
import concourse.bass as bass
import concourse.mybir as mybir
from concourse.bass_utils import run_bass_kernel_spmd

F32 = mybir.dt.float32
BF16 = mybir.dt.bfloat16
FP8 = mybir.dt.float8e4
AF = mybir.ActivationFunctionType
ALU = mybir.AluOpType

D = 2048
T = 1024
TH = 512
NKC = 16
NH = 16
DI = 6144
NIC = 48
GSZ = 8
NG = NIC // GSZ

ENGINES = ("sync", "tensor", "vector", "scalar", "gpsimd")


class Prog:
    def __init__(self):
        self.streams = {e: [] for e in ENGINES}
        self.count = {e: 0 for e in ENGINES}
        self.cc_count = 0

    def op(self, eng, fn, inc=0):
        self.count[eng] += inc
        self.streams[eng].append(("op", fn, inc))
        return (eng, self.count[eng])

    def dma(self, eng, out, in_):
        return self.op(eng, lambda E: E.dma_start(out=out, in_=in_), inc=16)

    def wait(self, eng, token):
        p, v = token
        if p == "cc":
            self.streams[eng].append(("waitcc", v))
            return
        assert v <= self.count[p], f"wait on future {p}={v}>{self.count[p]}"
        self.streams[eng].append(("wait", p, v))

    def collective(self, fn):
        self.cc_count += 1
        self.streams["gpsimd"].append(("cc", fn))
        return ("cc", self.cc_count)

    def token(self, eng):
        return (eng, self.count[eng])

    def replay(self, name, E, sems, cc_sem):
        for item in self.streams[name]:
            k = item[0]
            if k == "op":
                inst = item[1](E)
                if item[2]:
                    inst.then_inc(sems[name], item[2])
            elif k == "wait":
                E.wait_ge(sems[item[1]], item[2])
            elif k == "waitcc":
                E.wait_ge(cc_sem, item[1])
            else:
                item[1](E).then_inc(cc_sem, 1)


def host_consts():
    n = np.arange(128)
    ang = 2.0 * np.pi * np.outer(n, n) / 128.0
    consts = dict(
        cm=np.cos(ang).astype(ml_dtypes.bfloat16),
        sm=(-np.sin(ang)).astype(ml_dtypes.bfloat16),
        sel16=np.zeros((128, 256), dtype=ml_dtypes.bfloat16),
        selbc=np.zeros((16, 2048), dtype=ml_dtypes.bfloat16),
        negc16=np.full((16, 128), -1.0 / 2048.0, dtype=ml_dtypes.bfloat16),
        posc16=np.full((16, 1), 1.0 / 2048.0, dtype=ml_dtypes.bfloat16),
        ones1x128=np.ones((1, 128), dtype=ml_dtypes.bfloat16),
        ones_sc=np.ones((128, T), dtype=ml_dtypes.bfloat16),
    )
    for h in range(16):
        consts["sel16"][:, 16 * h + h] = 1.0
        consts["selbc"][h, 128 * h:128 * (h + 1)] = 1.0
    return consts


def build_program(taps=()):
    nc = bass.Bass("TRN2", target_bir_lowering=False)
    taps = set(taps)
    tap_list = []

    dx = nc.dram_tensor("x", [D, T], F32, kind="ExternalInput").ap()
    dwq = nc.dram_tensor("wq_t", [D, D], F32, kind="ExternalInput").ap()
    dwkv = nc.dram_tensor("wkv_t", [D, D], F32, kind="ExternalInput").ap()
    dwo = nc.dram_tensor("wo_t", [D, D], F32, kind="ExternalInput").ap()
    dgw = nc.dram_tensor("gw", [D, DI], F32, kind="ExternalInput").ap()
    duw = nc.dram_tensor("uw", [D, DI], F32, kind="ExternalInput").ap()
    ddw = nc.dram_tensor("dw", [DI, D], F32, kind="ExternalInput").ap()
    dgs = nc.dram_tensor("gscol", [128, NIC], F32, kind="ExternalInput").ap()
    dus = nc.dram_tensor("uscol", [128, NIC], F32, kind="ExternalInput").ap()
    dds = nc.dram_tensor("dscol", [128, 16], F32, kind="ExternalInput").ap()
    dmask = nc.dram_tensor("maskag", [128, 8], F32, kind="ExternalInput").ap()
    dcm = nc.dram_tensor("cm", [128, 128], BF16, kind="ExternalInput").ap()
    dsm = nc.dram_tensor("sm", [128, 128], BF16, kind="ExternalInput").ap()
    dsel = nc.dram_tensor("sel16", [128, 256], BF16, kind="ExternalInput").ap()
    dselbc = nc.dram_tensor("selbc", [16, 2048], BF16, kind="ExternalInput").ap()
    dnegc = nc.dram_tensor("negc16", [16, 128], BF16, kind="ExternalInput").ap()
    dposc = nc.dram_tensor("posc16", [16, 1], BF16, kind="ExternalInput").ap()
    dones1 = nc.dram_tensor("ones1x128", [1, 128], BF16, kind="ExternalInput").ap()
    dones_sc = nc.dram_tensor("ones_sc", [128, T], BF16, kind="ExternalInput").ap()
    dout = nc.dram_tensor("out", [D, T], F32, kind="ExternalOutput").ap()
    cc_in = nc.dram_tensor("cc_in", [128, 32], F32)
    cc_out = nc.dram_tensor("cc_out", [128 * 8, 32], F32)

    ctx = contextlib.ExitStack()
    sbuf = lambda nm, sh, dt: ctx.enter_context(nc.sbuf_tensor(nm, sh, dt))
    psum = lambda nm, sh: ctx.enter_context(nc.psum_tensor(nm, sh, F32))

    x_sb = sbuf("x_sb", [128, NKC * T], F32)
    z_sb = sbuf("z_sb", [128, 32 * T], BF16)
    pool = sbuf("pool", [128, 32 * T], BF16)
    cm_sb = sbuf("cm_sb", [128, 128], BF16)
    sm_sb = sbuf("sm_sb", [128, 128], BF16)
    sel_sb = sbuf("sel_sb", [128, 256], BF16)
    selbc_sb = sbuf("selbc_sb", [16, 2048], BF16)
    negc_sb = sbuf("negc_sb", [16, 128], BF16)
    posc_sb = sbuf("posc_sb", [16, 1], BF16)
    ones1_sb = sbuf("ones1_sb", [1, 128], BF16)
    mask_sb = sbuf("mask_sb", [128, 8], F32)
    gs_sb = sbuf("gs_sb", [128, NIC], F32)
    us_sb = sbuf("us_sb", [128, NIC], F32)
    ds_sb = sbuf("ds_sb", [128, 16], F32)
    rnk16 = sbuf("rnk16", [16, T], BF16)
    xb_t = sbuf("xb_t", [128, T], BF16)
    sq_t = sbuf("sq_t", [128, T], BF16)
    wmat = sbuf("wmat", [128, 16], BF16)
    bc2a = sbuf("bc2a", [128, T], BF16)
    sums_sb = sbuf("sums_sb", [128, 32], F32)
    carry_sb = sbuf("carry_sb", [128, 32], F32)
    nrows = z_sb[0:16, 8 * T:9 * T]
    nrows2 = z_sb[0:16, 9 * T:10 * T]
    rnq16 = pool[0:16, 12 * T:13 * T]
    rowbuf = pool[0:1, 16 * T:16 * T + 2 * T]
    poolf32 = pool.bitcast(F32)
    gath_sb = poolf32[:, 7680:7680 + 256]
    ones_sc_sb = pool[:, 11 * T:12 * T]

    ps_a = psum("ps_a", [128, T])
    ps_b = psum("ps_b", [128, T])
    ps_c = psum("ps_c", [128, T])
    ps_d = psum("ps_d", [128, T])
    PS = [ps_a, ps_b]

    P = Prog()

    def slot(i, half=None):
        if half is None:
            return pool[:, i * T:(i + 1) * T]
        return pool[:, i * T + half * TH: i * T + (half + 1) * TH]

    def zslot(i, half=None):
        if half is None:
            return z_sb[:, i * T:(i + 1) * T]
        return z_sb[:, i * T + half * TH: i * T + (half + 1) * TH]

    def xch(fc, half=None):
        if half is None:
            return x_sb[:, fc * T:(fc + 1) * T]
        return x_sb[:, fc * T + half * TH: fc * T + (half + 1) * TH]

    def tap(name, ap, shape, dt, deps):
        if name not in taps:
            return
        t = nc.dram_tensor("tap_" + name, shape, dt, kind="ExternalOutput").ap()
        tap_list.append(name)
        for d in deps:
            P.wait("sync", d)
        P.dma("sync", t, ap)

    ld_x = P.dma("sync", x_sb.rearrange("p (fc t) -> p fc t", t=T),
                 dx.rearrange("(fc p) t -> p fc t", p=128))
    for dst, src in ((cm_sb, dcm), (sm_sb, dsm), (sel_sb, dsel), (selbc_sb, dselbc),
                     (negc_sb, dnegc), (posc_sb, dposc), (ones1_sb, dones1),
                     (mask_sb, dmask),
                     (gs_sb, dgs), (us_sb, dus), (ds_sb, dds)):
        P.dma("sync", dst[:, :], src[:, :])
    ld_consts = P.token("sync")

    def wtile(buf, kc, hq):
        base = (16 + 8 * buf) * T + kc * TH + hq * 128
        return pool[:, base:base + 128]

    def stream_quarter(dw_, qn, buf, guard=None):
        if guard is not None:
            P.wait("gpsimd", guard)
        base = (16 + 8 * buf) * T
        dst = pool[:, base:base + 8 * T].rearrange("p (kc c) -> p kc c", c=TH)
        src = dw_.rearrange("(kc p) d -> p kc d", p=128)[:, :, TH * qn:TH * (qn + 1)]
        return P.dma("gpsimd", dst, src)

    P.wait("scalar", ld_x)
    P.wait("tensor", ld_consts)
    mm_prev = None
    for i in range(2 * NKC):
        fc, th = divmod(i, 2)
        cb = xb_t[:, (i % 2) * TH:((i % 2) + 1) * TH]
        if i >= 2:
            P.wait("scalar", mm_prev)
        ctk = P.op("scalar", lambda E, o=cb, s=xch(fc, th): E.copy(out=o, in_=s), inc=1)
        P.wait("tensor", ctk)
        mm_prev = P.op("tensor", lambda E, o=ps_c[0:16, th * TH:(th + 1) * TH],
                       l=sel_sb[:, 16 * fc:16 * fc + 16], r=cb,
                       s=(fc == 0), st=(fc == NKC - 1):
                       E.matmul(o, l, r, start=s, stop=st), inc=1)
    P.wait("vector", mm_prev)
    ev1 = P.op("vector", lambda E: E.tensor_copy(nrows, ps_c[0:16, :]), inc=1)
    P.wait("tensor", ev1)
    bcmm = None
    for th in range(2):
        bcmm = P.op("tensor", lambda E, o=ps_a[:, th * TH:(th + 1) * TH],
                    r=nrows[:, th * TH:(th + 1) * TH]:
                    E.matmul(o, negc_sb[:, :], r, start=True, stop=True), inc=(th == 1))
    bneg1 = zslot(31)
    P.wait("scalar", bcmm)
    bneg1_tok = P.op("scalar", lambda E: E.copy(out=bneg1, in_=ps_a[:, :]), inc=1)
    x1_tok = None
    x1_chunk = {}
    for fc in range(NKC):
        P.wait("vector", bneg1_tok)
        x1_chunk[fc] = P.op("vector", lambda E, o=slot(fc), a=xch(fc):
                            E.tensor_add(o, a, bneg1), inc=1)
        x1_tok = x1_chunk[fc]
    tap("x1", pool[:, 0:NKC * T], [128, NKC * T], BF16, [x1_tok])

    sqbufs = [xb_t, sq_t]

    def proj_head(h, wbuf, psn, psz, first=False):
        hq = h % 4
        for th in range(2):
            for kc in range(NKC):
                if first and th == 0:
                    P.wait("tensor", x1_chunk[kc])
                last = kc == NKC - 1
                P.op("tensor", lambda E, o=psz[:, th * TH:(th + 1) * TH],
                     l=wtile(wbuf, kc, hq), r=slot(kc, th), s=(kc == 0), st=last:
                     E.matmul(o, l, r, start=s, stop=st), inc=1 if last else 0)
        mm_tok = P.token("tensor")
        P.wait("scalar", mm_tok)
        zdst = zslot(h + (0 if psn is ps_c else 16))
        zev = P.op("scalar", lambda E, o=zdst, i=psz[:, :]: E.copy(out=o, in_=i), inc=1)
        sqb = sqbufs[h % 2]
        sqev = P.op("scalar", lambda E, o=sqb[:, :], i=psz[:, :]:
                    E.activation(out=o, in_=i, func=AF.Square), inc=1)
        P.wait("tensor", sqev)
        nmm = None
        for th in range(2):
            nmm = P.op("tensor", lambda E, o=psn[0:16, th * TH:(th + 1) * TH],
                       l=sel_sb[:, 16 * h:16 * h + 16], r=sqb[:, th * TH:(th + 1) * TH],
                       s=(h == 0), st=(h == NH - 1):
                       E.matmul(o, l, r, start=s, stop=st),
                       inc=(1 if (h == NH - 1 and th == 1) else 0))
        return mm_tok, zev, sqev, nmm

    zf32 = z_sb.bitcast(F32)
    def finalize_norm(psn, scratch, dst):
        tok = P.op("vector", lambda E: E.reciprocal(scratch, psn[0:16, :]), inc=1)
        P.wait("scalar", tok)
        return P.op("scalar", lambda E: E.activation(out=dst, in_=scratch,
                                                     func=AF.Sqrt, scale=1.0 / 128.0), inc=1)

    qtr_done = {}
    pref = {("kv", 0): stream_quarter(dwkv, 0, 0), ("kv", 1): stream_quarter(dwkv, 1, 1)}
    guards = {}
    kv_nmm = None
    for h in range(NH):
        qn, buf = h // 4, (h // 4) % 2
        if h % 4 == 0:
            P.wait("tensor", pref[("kv", qn)])
        if h >= 2:
            P.wait("tensor", guards[h - 2])
        mm_tok, zev, sqev, kv_nmm = proj_head(h, buf, ps_d, PS[h % 2], first=(h == 0))
        guards[h] = sqev
        if h % 4 == 3:
            qtr_done[("kv", qn)] = mm_tok
            if qn < 3:
                pref[("kv", qn + 1)] = pref.get(("kv", qn + 1)) or \
                    stream_quarter(dwkv, qn + 1, (qn + 1) % 2,
                                   guard=qtr_done.get(("kv", qn - 1)))
    P.wait("vector", kv_nmm)
    rnk_tok = finalize_norm(ps_d, zf32[0:16, 0:T], rnk16[:, :])
    tap("rnk16", rnk16[:, :], [16, T], BF16, [rnk_tok])

    pref[("q", 0)] = stream_quarter(dwq, 0, 0, guard=qtr_done[("kv", 2)])
    pref[("q", 1)] = stream_quarter(dwq, 1, 1, guard=qtr_done[("kv", 3)])
    P.wait("tensor", rnk_tok)
    P.wait("tensor", kv_nmm)
    prev_bcev = None
    cst = None
    sums_ev = None
    cc_send = None
    cc_tok = None
    smm = None
    q_nmm = None
    def sums_head(hs):
        bcmm = None
        for th in range(2):
            bcmm = P.op("tensor", lambda E, o=ps_d[:, th * TH:(th + 1) * TH],
                        l=selbc_sb[:, 128 * hs:128 * (hs + 1)],
                        r=rnk16[:, th * TH:(th + 1) * TH]:
                        E.matmul(o, l, r, start=True, stop=True), inc=(th == 1))
        P.wait("scalar", bcmm)
        bcev = P.op("scalar", lambda E: E.copy(out=bc2a[:, :], in_=ps_d[:, :]), inc=1)
        P.wait("vector", bcev)
        cst = P.op("vector", lambda E, z=zslot(16 + hs), hh=hs: E.scalar_tensor_tensor(
            bc2a[:, :], z, 1.0, bc2a[:, :], ALU.mult, ALU.mult,
            accum_out=wmat[:, hh:hh + 1]), inc=1)
        return bcev, cst

    for h in range(NH):
        qn, buf = h // 4, (h // 4) % 2
        if h < 8:
            for hs in (2 * h, 2 * h + 1):
                if prev_bcev is not None:
                    P.wait("tensor", prev_bcev)
                prev_bcev, cst = sums_head(hs)
        if h == 8:
            tap("wmat", wmat[:, :], [128, 16], BF16, [cst])
            P.wait("tensor", cst)
            P.op("tensor", lambda E: E.matmul(ps_d[:, 0:16], cm_sb[:, :], wmat[:, :],
                                              start=True, stop=True), inc=0)
            smm = P.op("tensor", lambda E: E.matmul(ps_d[:, 16:32], sm_sb[:, :], wmat[:, :],
                                                    start=True, stop=True), inc=1)
            P.wait("vector", smm)
            sums_ev = P.op("vector", lambda E: E.tensor_copy(sums_sb[:, :], ps_d[:, 0:32]), inc=1)
            P.wait("sync", sums_ev)
            cc_send = P.dma("sync", cc_in.ap()[:, :], sums_sb[:, :])
            P.wait("gpsimd", cc_send)
            cc_tok = P.collective(lambda E: E.collective_compute(
                "AllGather", ALU.bypass, replica_groups=[list(range(8))],
                ins=[cc_in.ap().opt()], outs=[cc_out.ap().opt()]))
        if h % 4 == 0:
            P.wait("tensor", pref[("q", qn)])
        P.wait("tensor", guards[NH - 2 + h] if h < 2 else guards[NH + h - 2])
        mm_tok, zev, sqev, q_nmm = proj_head(h, buf, ps_c, PS[h % 2])
        guards[NH + h] = sqev
        if h % 4 == 3:
            qtr_done[("q", qn)] = mm_tok
            if qn < 3:
                pref[("q", qn + 1)] = pref.get(("q", qn + 1)) or \
                    stream_quarter(dwq, qn + 1, (qn + 1) % 2,
                                   guard=qtr_done.get(("q", qn - 1)))
    P.wait("vector", q_nmm)
    rnq_tok = finalize_norm(ps_c, poolf32[0:16, 6656:6656 + T], rnq16)
    tap("rnq16", rnq16, [16, T], BF16, [rnq_tok])
    tap("zq0", zslot(0), [128, T], BF16, [guards[NH]])
    tap("zkv0", zslot(16), [128, T], BF16, [guards[0]])
    tap("sums", sums_sb[:, :], [128, 32], F32, [sums_ev])
    P.wait("sync", qtr_done[("q", 3)])
    ones_ld = P.dma("sync", ones_sc_sb, dones_sc[:, :])

    P.wait("sync", cc_tok)
    P.wait("sync", qtr_done[("q", 3)])
    gl = None
    for r in range(8):
        gl = P.dma("sync", gath_sb[:, 32 * r:32 * (r + 1)],
                   cc_out.ap()[128 * r:128 * (r + 1), :])
    P.wait("vector", gl)
    P.op("vector", lambda E: E.memset(carry_sb[:, :], 0.0), inc=0)
    carry_tok = None
    for r in range(8):
        carry_tok = P.op("vector", lambda E, rr=r: E.scalar_tensor_tensor(
            carry_sb[:, :], gath_sb[:, 32 * rr:32 * (rr + 1)], mask_sb[:, rr:rr + 1],
            carry_sb[:, :], ALU.mult, ALU.add), inc=1)
    tap("carry", carry_sb[:, :], [128, 32], F32, [carry_tok])

    t_cr, t_ci = slot(2), slot(3)
    t_qr, t_qi = slot(4), slot(5)
    t_vr, t_vi = slot(6), slot(7)
    t_sc, t_sc2 = slot(8), slot(9)
    zs2 = [slot(13), slot(10)]
    pref[("o", 0)] = stream_quarter(dwo, 0, 0, guard=qtr_done[("q", 2)])
    pref[("o", 1)] = stream_quarter(dwo, 1, 1, guard=qtr_done[("q", 3)])
    flush1 = {}
    prev_tkv = None
    prev_tq = None
    prev_vev = None
    prev_q1 = None
    prev_s2 = None
    wo1_dc = 0
    wo1_flush = None
    wo1_mm = None

    def wo_half1_step(vev_tok, q2_tok):
        nonlocal wo1_dc, wo1_flush, wo1_mm
        if wo1_dc >= 16:
            return
        dc = wo1_dc
        qn, dcq = dc // 4, dc % 4
        P.wait("tensor", pref[("o", qn)])
        if wo1_flush is not None:
            P.wait("tensor", wo1_flush)
        else:
            P.wait("tensor", q2_tok)
        mmv = None
        for th in range(2):
            for hh in range(8):
                last = hh == 7
                mmv = P.op("tensor", lambda E, o=ps_d[:, th * TH:(th + 1) * TH],
                           l=wtile(qn % 2, hh, dcq), r=zslot(16 + hh, th),
                           s=(hh == 0), st=last:
                           E.matmul(o, l, r, start=s, stop=st),
                           inc=(1 if (last and th == 1) else 0))
        wo1_mm = mmv
        P.wait("vector", mmv)
        wo1_flush = P.op("vector", lambda E, o=xch(dc), i=ps_d[:, :]:
                         E.tensor_add(o, o, i), inc=1)
        flush1[dc] = wo1_flush
        wo1_dc += 1
        if wo1_dc == 4:
            pref[("o", 2)] = stream_quarter(dwo, 2, 0, guard=wo1_mm)
        if wo1_dc == 8:
            pref[("o", 3)] = stream_quarter(dwo, 3, 1, guard=wo1_mm)
        if wo1_dc == 12:
            pref[("o2", 0)] = stream_quarter(dwo, 0, 0, guard=wo1_mm)
        if wo1_dc == 16:
            pref[("o2", 1)] = stream_quarter(dwo, 1, 1, guard=wo1_mm)

    for h in range(NH):
        P.wait("tensor", rnq_tok)
        P.wait("tensor", carry_tok)
        zs = zs2[h % 2]
        if h == 0:
            P.wait("tensor", qtr_done[("q", 3)])
        if prev_vev is not None:
            P.wait("tensor", prev_vev)
        mm1 = mm2 = None
        for th in range(2):
            mm1 = P.op("tensor", lambda E, o=ps_a[:, th * TH:(th + 1) * TH],
                       l=selbc_sb[:, 128 * h:128 * (h + 1)],
                       r=rnk16[:, th * TH:(th + 1) * TH]:
                       E.matmul(o, l, r, start=True, stop=True), inc=(th == 1))
        if prev_tq is not None:
            P.wait("tensor", prev_tq)
        for th in range(2):
            mm2 = P.op("tensor", lambda E, o=ps_b[:, th * TH:(th + 1) * TH],
                       l=selbc_sb[:, 128 * h:128 * (h + 1)],
                       r=rnq16[:, th * TH:(th + 1) * TH]:
                       E.matmul(o, l, r, start=True, stop=True), inc=(th == 1))
        P.wait("vector", mm1)
        tkv = P.op("vector", lambda E, z=zslot(16 + h), o=zs:
                   E.tensor_tensor(o, z, ps_a[:, :], ALU.mult), inc=1)
        P.wait("tensor", tkv)
        if prev_q1 is not None:
            P.wait("tensor", prev_q1)
        f1 = f2 = None
        for th in range(2):
            f1 = P.op("tensor", lambda E, o=ps_c[:, th * TH:(th + 1) * TH], rr=(th * TH, (th + 1) * TH), z=zs:
                      E.matmul(o, cm_sb[:, :], z[:, rr[0]:rr[1]], start=True, stop=True), inc=(th == 1))
        if h >= 8:
            if wo1_flush is not None:
                P.wait("tensor", wo1_flush)
        for th in range(2):
            f2 = P.op("tensor", lambda E, o=ps_d[:, th * TH:(th + 1) * TH], rr=(th * TH, (th + 1) * TH), z=zs:
                      E.matmul(o, sm_sb[:, :], z[:, rr[0]:rr[1]], start=True, stop=True), inc=(th == 1))
        if h == 0:
            P.wait("vector", ones_ld)
        P.wait("vector", f1)
        s1 = P.op("vector", lambda E, ic=carry_sb[:, h:h + 1]:
                  E.tensor_tensor_scan(t_cr, ones_sc_sb, ps_c[:, :], ic,
                                       ALU.mult, ALU.add), inc=1)
        P.wait("vector", f2)
        s2 = P.op("vector", lambda E, ic=carry_sb[:, 16 + h:16 + h + 1]:
                  E.tensor_tensor_scan(t_ci, ones_sc_sb, ps_d[:, :], ic,
                                       ALU.mult, ALU.add), inc=1)
        zsq = zs2[(h + 1) % 2]
        P.wait("vector", mm2)
        tq = P.op("vector", lambda E, z=zslot(h), o=zsq:
                  E.tensor_tensor(o, z, ps_b[:, :], ALU.mult), inc=1)
        P.wait("tensor", tq)
        P.wait("tensor", s1)
        g1 = g2 = None
        for th in range(2):
            g1 = P.op("tensor", lambda E, o=ps_c[:, th * TH:(th + 1) * TH], rr=(th * TH, (th + 1) * TH), z=zsq:
                      E.matmul(o, cm_sb[:, :], z[:, rr[0]:rr[1]], start=True, stop=True), inc=(th == 1))
        P.wait("tensor", s2)
        for th in range(2):
            g2 = P.op("tensor", lambda E, o=ps_d[:, th * TH:(th + 1) * TH], rr=(th * TH, (th + 1) * TH), z=zsq:
                      E.matmul(o, sm_sb[:, :], z[:, rr[0]:rr[1]], start=True, stop=True), inc=(th == 1))
        P.wait("scalar", g1)
        q1 = P.op("scalar", lambda E: E.copy(out=t_qr, in_=ps_c[:, :]), inc=1)
        P.wait("scalar", g2)
        q2 = P.op("scalar", lambda E: E.copy(out=t_qi, in_=ps_d[:, :]), inc=1)
        P.wait("gpsimd", q2)
        P.wait("gpsimd", s2)
        gp1 = P.op("gpsimd", lambda E: E.tensor_tensor(t_sc, t_qi, t_ci, ALU.mult), inc=1)
        gp2 = P.op("gpsimd", lambda E: E.tensor_tensor(t_vi, t_qi, t_cr, ALU.mult), inc=1)
        P.wait("vector", q1)
        P.op("vector", lambda E: E.tensor_mul(t_vr, t_qr, t_cr), inc=0)
        P.wait("vector", gp1)
        P.op("vector", lambda E: E.tensor_sub(t_vr, t_vr, t_sc), inc=0)
        P.op("vector", lambda E: E.tensor_mul(t_sc2, t_qr, t_ci), inc=0)
        P.wait("vector", gp2)
        prod = P.op("vector", lambda E: E.tensor_add(t_vi, t_sc2, t_vi), inc=1)
        P.wait("tensor", prod)
        vm = None
        for th in range(2):
            P.op("tensor", lambda E, o=ps_a[:, th * TH:(th + 1) * TH], rr=(th * TH, (th + 1) * TH):
                 E.matmul(o, cm_sb[:, :], t_vr[:, rr[0]:rr[1]], start=True, stop=False), inc=0)
            vm = P.op("tensor", lambda E, o=ps_a[:, th * TH:(th + 1) * TH], rr=(th * TH, (th + 1) * TH):
                      E.matmul(o, sm_sb[:, :], t_vi[:, rr[0]:rr[1]], start=False, stop=True), inc=(th == 1))
        P.wait("scalar", vm)
        vev = P.op("scalar", lambda E, o=zslot(16 + h): E.activation(
            out=o, in_=ps_a[:, :], func=AF.Copy, scale=1.0 / 128.0), inc=1)
        prev_tkv, prev_tq, prev_vev, prev_q1, prev_s2 = tkv, tq, vev, q1, s2
        if h >= 8:
            wo_half1_step(vev, q2)
            wo_half1_step(vev, q2)
        if h == 0:
            tap("kvc0", t_cr, [128, T], BF16, [s1])
            tap("qfn0", t_qr, [128, T], BF16, [q1])
            tap("v0", zslot(16), [128, T], BF16, [vev])

    flush = {}
    nfl = 0
    last_flush = None
    mm_m = mm_s = None
    ln_prev = None
    h2mm = {}
    for qn in range(4):
        buf = qn % 2
        if qn == 2:
            pref[("o2", 2)] = stream_quarter(dwo, 2, 0, guard=h2mm[0])
        if qn == 3:
            pref[("o2", 3)] = stream_quarter(dwo, 3, 1, guard=h2mm[1])
        P.wait("tensor", pref[("o2", qn)])
        for dcq in range(4):
            dc = 4 * qn + dcq
            psd = PS[dc % 2]
            if nfl >= 2:
                P.wait("tensor", flush[nfl - 2])
            elif nfl == 0:
                P.wait("tensor", prev_vev)
            mmv = None
            for th in range(2):
                for hh in range(8, NH):
                    last = hh == NH - 1
                    mmv = P.op("tensor", lambda E, o=psd[:, th * TH:(th + 1) * TH],
                               l=wtile(buf, hh, dcq), r=zslot(16 + hh, th),
                               s=(hh == 8), st=last:
                               E.matmul(o, l, r, start=s, stop=st),
                               inc=(1 if (last and th == 1) else 0))
            P.wait("vector", mmv)
            P.wait("vector", flush1[dc])
            ft = P.op("vector", lambda E, o=xch(dc), i=psd[:, :]:
                      E.tensor_add(o, o, i), inc=1)
            flush[nfl] = ft
            nfl += 1
            last_flush = ft
            h2mm[qn] = mmv
            cb = [xb_t, sq_t][dc % 2]
            P.wait("scalar", ft)
            if ln_prev is not None:
                P.wait("scalar", ln_prev)
            c1 = P.op("scalar", lambda E, o=cb[:, :], s=xch(dc): E.copy(out=o, in_=s), inc=1)
            c2 = P.op("scalar", lambda E, o=bc2a[:, :], s=xch(dc):
                      E.activation(out=o, in_=s, func=AF.Square), inc=1)
            P.wait("tensor", c2)
            for th in range(2):
                mm_m = P.op("tensor", lambda E, o=ps_c[0:16, th * TH:(th + 1) * TH],
                            l=sel_sb[:, 16 * dc:16 * dc + 16], r=cb[:, th * TH:(th + 1) * TH],
                            s=(dc == 0), st=(dc == NKC - 1):
                            E.matmul(o, l, r, start=s, stop=st), inc=0)
                mm_s = P.op("tensor", lambda E, o=ps_d[0:16, th * TH:(th + 1) * TH],
                            l=sel_sb[:, 16 * dc:16 * dc + 16], r=bc2a[:, th * TH:(th + 1) * TH],
                            s=(dc == 0), st=(dc == NKC - 1):
                            E.matmul(o, l, r, start=s, stop=st), inc=(1 if th == 1 else 0))
            ln_prev = mm_s
    tap("xattn", x_sb[:, :], [128, NKC * T], F32, [last_flush])

    P.wait("vector", mm_s)
    P.op("vector", lambda E: E.tensor_copy(nrows, ps_c[0:16, :]), inc=0)
    e_s = P.op("vector", lambda E: E.tensor_copy(nrows2, ps_d[0:16, :]), inc=1)
    P.wait("tensor", e_s)
    r_m = r_s = None
    for th in range(2):
        r_m = P.op("tensor", lambda E, o=ps_a[0:1, th * TH:(th + 1) * TH],
                   r=nrows[:, th * TH:(th + 1) * TH]:
                   E.matmul(o, posc_sb[:, :], r, start=True, stop=True), inc=(th == 1))
        r_s = P.op("tensor", lambda E, o=ps_b[0:1, th * TH:(th + 1) * TH],
                   r=nrows2[:, th * TH:(th + 1) * TH]:
                   E.matmul(o, posc_sb[:, :], r, start=True, stop=True), inc=(th == 1))
    P.wait("vector", r_s)
    rowm = poolf32[0:1, 9216:9216 + T]
    rowv = poolf32[0:1, 10240:10240 + T]
    rowr = poolf32[0:1, 11264:11264 + T]
    P.op("vector", lambda E: E.tensor_copy(rowm, ps_a[0:1, :]), inc=0)
    P.op("vector", lambda E: E.tensor_mul(rowv, ps_a[0:1, :], rowm), inc=0)
    P.op("vector", lambda E: E.tensor_sub(rowv, ps_b[0:1, :], rowv), inc=0)
    P.op("vector", lambda E: E.tensor_scalar_add(rowv, rowv, 1e-5), inc=0)
    rec = P.op("vector", lambda E: E.reciprocal(rowr, rowv), inc=1)
    nm2 = P.op("vector", lambda E: E.tensor_scalar_mul(rowbuf[0:1, 0:T], rowm, -1.0), inc=1)
    P.wait("scalar", rec)
    r2t = P.op("scalar", lambda E: E.activation(out=rowbuf[0:1, T:2 * T],
                                                in_=rowr, func=AF.Sqrt), inc=1)
    P.wait("tensor", r2t)
    P.wait("tensor", nm2)
    b_m = b_r = None
    for th in range(2):
        b_m = P.op("tensor", lambda E, o=ps_c[:, th * TH:(th + 1) * TH],
                   r=rowbuf[0:1, th * TH:(th + 1) * TH]:
                   E.matmul(o, ones1_sb[:, :], r, start=True, stop=True), inc=(th == 1))
        b_r = P.op("tensor", lambda E, o=ps_d[:, th * TH:(th + 1) * TH],
                   r=rowbuf[0:1, T + th * TH:T + (th + 1) * TH]:
                   E.matmul(o, ones1_sb[:, :], r, start=True, stop=True), inc=(th == 1))
    bneg2, br2 = zslot(31), zslot(30)
    P.wait("scalar", b_m)
    P.op("scalar", lambda E: E.copy(out=bneg2, in_=ps_c[:, :]), inc=0)
    eb2 = P.op("scalar", lambda E: E.copy(out=br2, in_=ps_d[:, :]), inc=1)
    x2_tok = None
    x2_chunk = {}
    for fc in range(NKC):
        P.wait("vector", eb2)
        P.op("vector", lambda E, o=slot(fc), a=xch(fc): E.tensor_add(o, a, bneg2), inc=0)
        x2_chunk[fc] = P.op("vector", lambda E, o=slot(fc): E.tensor_mul(o, o, br2), inc=1)
        x2_tok = x2_chunk[fc]
    tap("x2", pool[:, 0:NKC * T], [128, NKC * T], BF16, [x2_tok])

    z8 = z_sb.bitcast(FP8)

    def dgrp(i):
        return z_sb[:, 2048 * i:2048 * (i + 1)]

    def gbuf(b):
        return z_sb[:, 16384 + 2048 * b:16384 + 2048 * (b + 1)]

    def ubuf(b):
        return z_sb[:, 20480 + 2048 * b:20480 + 2048 * (b + 1)]

    def g8(b):
        return z8[:, 49152 + 2048 * b:49152 + 2048 * (b + 1)]

    def u8(b):
        return z8[:, 53248 + 2048 * b:53248 + 2048 * (b + 1)]

    def d8(b):
        return z8[:, 57344 + 2048 * b:57344 + 2048 * (b + 1)]

    gsrc = dgw.rearrange("(kc p) i -> p kc i", p=128)
    usrc = duw.rearrange("(kc p) i -> p kc i", p=128)

    def load8(ic, b):
        P.dma("gpsimd", g8(b).rearrange("p (kc c) -> p kc c", c=128),
              gsrc[:, :, 128 * ic:128 * (ic + 1)])
        P.dma("gpsimd", u8(b).rearrange("p (kc c) -> p kc c", c=128),
              usrc[:, :, 128 * ic:128 * (ic + 1)])
        return P.dma("gpsimd", d8(b), ddw[128 * ic:128 * (ic + 1), :])

    def upcast(ic):
        b, icg = ic % 2, ic % GSZ
        g = ic // GSZ
        P.wait("gpsimd", ld8[ic])
        if ic >= 2:
            P.wait("gpsimd", fuse_tok[ic - 2])
        if g > 0:
            P.wait("gpsimd", dmm_done[g - 1])
        P.dma("gpsimd", gbuf(b), g8(b))
        P.dma("gpsimd", ubuf(b), u8(b))
        return P.dma("gpsimd", dgrp(icg), d8(b))

    P.wait("gpsimd", P.token("tensor"))
    ld8 = {0: load8(0, 0), 1: load8(1, 1)}
    upc = {}
    fuse_tok = {}
    dmm_done = {}
    flf = {}
    silu_tok = {}
    nffl = 0
    last_ffn = None
    st_out = None
    upc[0] = upcast(0)
    GPS = [ps_a, ps_c]
    UPS = [ps_b, ps_d]
    for g in range(NG):
        for icg in range(GSZ):
            ic = g * GSZ + icg
            b = ic % 2
            if ic + 2 < NIC:
                ld8[ic + 2] = load8(ic + 2, b)
            if ic + 1 < NIC and (ic + 1) % GSZ != 0:
                upc[ic + 1] = upcast(ic + 1)
            P.wait("tensor", upc[ic])
            if ic == 0:
                P.wait("tensor", eb2)
            if ic >= 2:
                P.wait("tensor", fuse_tok[ic - 2])
            if icg == 0 and g > 0:
                P.wait("tensor", flf[nffl - 1])
            mg = mu = None
            psg, psu = GPS[ic % 2], UPS[ic % 2]
            for th in range(2):
                for kc in range(NKC):
                    if ic == 0 and th == 0:
                        P.wait("tensor", x2_chunk[kc])
                    last = kc == NKC - 1
                    mg = P.op("tensor", lambda E, o=psg[:, th * TH:(th + 1) * TH],
                              l=gbuf(b)[:, 128 * kc:128 * (kc + 1)], r=slot(kc, th),
                              s=(kc == 0), st=last:
                              E.matmul(o, l, r, start=s, stop=st),
                              inc=(1 if (last and th == 1) else 0))
            for th in range(2):
                for kc in range(NKC):
                    last = kc == NKC - 1
                    mu = P.op("tensor", lambda E, o=psu[:, th * TH:(th + 1) * TH],
                              l=ubuf(b)[:, 128 * kc:128 * (kc + 1)], r=slot(kc, th),
                              s=(kc == 0), st=last:
                              E.matmul(o, l, r, start=s, stop=st),
                              inc=(1 if (last and th == 1) else 0))
            gact = [xb_t[:, :], zslot(30)][ic % 2]
            uact = [sq_t[:, :], zslot(31)][ic % 2]
            P.wait("scalar", mg)
            sg = P.op("scalar", lambda E, i=ic, p=psg, o=gact: E.activation(
                out=o, in_=p[:, :], func=AF.Silu, scale=gs_sb[:, i:i + 1]), inc=1)
            silu_tok[ic] = sg
            P.wait("scalar", mu)
            su = P.op("scalar", lambda E, i=ic, p=psu, o=uact: E.activation(
                out=o, in_=p[:, :], func=AF.Copy, scale=us_sb[:, i:i + 1]), inc=1)
            P.wait("vector", su)
            fuse_tok[ic] = P.op("vector", lambda E, o=slot(16 + icg), a=gact, u=uact:
                                E.tensor_mul(o, a, u), inc=1)
        dlast = None
        for dc in range(16):
            psd = [ps_c, ps_d][dc % 2]
            if nffl >= 2:
                P.wait("tensor", flf[nffl - 2])
            if dc < 2:
                P.wait("tensor", silu_tok[g * GSZ + GSZ - 1])
                P.wait("tensor", fuse_tok[g * GSZ + GSZ - 1])
                P.wait("tensor", upc[g * GSZ + GSZ - 1])
            mmv = None
            for th in range(2):
                for icg in range(GSZ):
                    last = icg == GSZ - 1
                    mmv = P.op("tensor", lambda E, o=psd[:, th * TH:(th + 1) * TH],
                               l=dgrp(icg)[:, 128 * dc:128 * (dc + 1)],
                               r=slot(16 + icg, th), s=(icg == 0), st=last:
                               E.matmul(o, l, r, start=s, stop=st),
                               inc=(1 if (last and th == 1) else 0))
            dlast = mmv
            P.wait("vector", mmv)
            ft = P.op("vector", lambda E, o=xch(dc), i=psd[:, :], d=dc:
                      E.scalar_tensor_tensor(o, i, ds_sb[:, d:d + 1], o,
                                             ALU.mult, ALU.add), inc=1)
            flf[nffl] = ft
            nffl += 1
            last_ffn = ft
            if g == NG - 1:
                P.wait("sync", ft)
                st_out = P.dma("sync", dout.rearrange("(fc p) t -> p fc t", p=128)[:, dc, :],
                               xch(dc))
        dmm_done[g] = dlast
        if (g + 1) * GSZ < NIC:
            upc[(g + 1) * GSZ] = upcast((g + 1) * GSZ)
    P.wait("sync", st_out)

    with contextlib.ExitStack() as es:
        sems = {e: es.enter_context(nc.semaphore("sem_" + e)) for e in ENGINES}
        cc_sem = es.enter_context(nc.semaphore("sem_cc"))
        with nc.Block() as block:
            block.sync(lambda E: P.replay("sync", E, sems, cc_sem))
            block.tensor(lambda E: P.replay("tensor", E, sems, cc_sem))
            block.vector(lambda E: P.replay("vector", E, sems, cc_sem))
            block.scalar(lambda E: P.replay("scalar", E, sems, cc_sem))
            block.gpsimd(lambda E: P.replay("gpsimd", E, sems, cc_sem))
    ctx.close()
    return nc, tap_list


_BUILD_CACHE = {}


def _get_program(taps=()):
    key = tuple(sorted(taps))
    if key not in _BUILD_CACHE:
        _BUILD_CACHE[key] = build_program(taps)
    return _BUILD_CACHE[key]


def make_in_maps(L0_attn_x, L0_attn_W_q, L0_attn_W_kv, L0_attn_W_o,
                 L0_ffn_gate_w, L0_ffn_up_w, L0_ffn_down_w,
                 L0_ffn_gate_scale, L0_ffn_up_scale, L0_ffn_down_scale):
    x = np.asarray(L0_attn_x, dtype=np.float32)
    wq_t = np.ascontiguousarray(np.asarray(L0_attn_W_q, dtype=np.float32).T)
    wkv_t = np.ascontiguousarray(np.asarray(L0_attn_W_kv, dtype=np.float32).T)
    wo_t = np.ascontiguousarray(np.asarray(L0_attn_W_o, dtype=np.float32).T)
    gw = np.ascontiguousarray(np.asarray(L0_ffn_gate_w, dtype=np.float32))
    uw = np.ascontiguousarray(np.asarray(L0_ffn_up_w, dtype=np.float32))
    dw = np.ascontiguousarray(np.asarray(L0_ffn_down_w, dtype=np.float32))
    gscol = np.ascontiguousarray(np.asarray(L0_ffn_gate_scale, np.float32).reshape(NIC, 128).T)
    uscol = np.ascontiguousarray(np.asarray(L0_ffn_up_scale, np.float32).reshape(NIC, 128).T)
    dscol = np.ascontiguousarray(np.asarray(L0_ffn_down_scale, np.float32).reshape(16, 128).T)
    consts = host_consts()
    in_maps = []
    for c in range(8):
        b, hh = divmod(c, 2)
        xs = np.ascontiguousarray(x[b, T * hh:T * (hh + 1), :].T)
        mask = np.zeros((128, 8), dtype=np.float32)
        if hh == 1:
            mask[:, c - 1] = 1.0
        m = dict(x=xs, wq_t=wq_t, wkv_t=wkv_t, wo_t=wo_t, gw=gw, uw=uw, dw=dw,
                 gscol=gscol, uscol=uscol, dscol=dscol, maskag=mask, **consts)
        in_maps.append(m)
    return in_maps


def kernel(L0_attn_x, L0_attn_W_q, L0_attn_W_kv, L0_attn_W_o,
           L0_ffn_gate_w, L0_ffn_up_w, L0_ffn_down_w,
           L0_ffn_gate_scale, L0_ffn_up_scale, L0_ffn_down_scale):
    nc, _ = _get_program()
    in_maps = make_in_maps(L0_attn_x, L0_attn_W_q, L0_attn_W_kv, L0_attn_W_o,
                           L0_ffn_gate_w, L0_ffn_up_w, L0_ffn_down_w,
                           L0_ffn_gate_scale, L0_ffn_up_scale, L0_ffn_down_scale)
    res = run_bass_kernel_spmd(nc, in_maps, core_ids=list(range(8)))
    out = np.empty((4, 2048, 2048), dtype=np.float32)
    for c in range(8):
        b, hh = divmod(c, 2)
        out[b, T * hh:T * (hh + 1), :] = np.asarray(res.results[c]["out"]).T
    return out


# version 08
# speedup vs baseline: 1.1543x; 1.0449x over previous
import contextlib
import numpy as np
import ml_dtypes
import concourse.bass as bass
import concourse.mybir as mybir
from concourse.bass_utils import run_bass_kernel_spmd

F32 = mybir.dt.float32
BF16 = mybir.dt.bfloat16
FP8 = mybir.dt.float8e4
AF = mybir.ActivationFunctionType
ALU = mybir.AluOpType

D = 2048
T = 1024
TH = 512
NKC = 16
NH = 16
DI = 6144
NIC = 48
GSZ = 8
NG = NIC // GSZ

ENGINES = ("sync", "tensor", "vector", "scalar", "gpsimd")


class Prog:
    def __init__(self):
        self.streams = {e: [] for e in ENGINES}
        self.count = {e: 0 for e in ENGINES}
        self.cc_count = 0

    def op(self, eng, fn, inc=0):
        self.count[eng] += inc
        self.streams[eng].append(("op", fn, inc))
        return (eng, self.count[eng])

    def dma(self, eng, out, in_):
        return self.op(eng, lambda E: E.dma_start(out=out, in_=in_), inc=16)

    def wait(self, eng, token):
        p, v = token
        if p == "cc":
            self.streams[eng].append(("waitcc", v))
            return
        assert v <= self.count[p], f"wait on future {p}={v}>{self.count[p]}"
        self.streams[eng].append(("wait", p, v))

    def collective(self, fn):
        self.cc_count += 1
        self.streams["gpsimd"].append(("cc", fn))
        return ("cc", self.cc_count)

    def token(self, eng):
        return (eng, self.count[eng])

    def replay(self, name, E, sems, cc_sem):
        for item in self.streams[name]:
            k = item[0]
            if k == "op":
                inst = item[1](E)
                if item[2]:
                    inst.then_inc(sems[name], item[2])
            elif k == "wait":
                E.wait_ge(sems[item[1]], item[2])
            elif k == "waitcc":
                E.wait_ge(cc_sem, item[1])
            else:
                item[1](E).then_inc(cc_sem, 1)


def host_consts():
    n = np.arange(128)
    ang = 2.0 * np.pi * np.outer(n, n) / 128.0
    consts = dict(
        cm=np.cos(ang).astype(ml_dtypes.bfloat16),
        sm=(-np.sin(ang)).astype(ml_dtypes.bfloat16),
        sel16=np.zeros((128, 256), dtype=ml_dtypes.bfloat16),
        selbc=np.zeros((16, 2048), dtype=ml_dtypes.bfloat16),
        negc16=np.full((16, 128), -1.0 / 2048.0, dtype=ml_dtypes.bfloat16),
        posc16=np.full((16, 1), 1.0 / 2048.0, dtype=ml_dtypes.bfloat16),
        ones1x128=np.ones((1, 128), dtype=ml_dtypes.bfloat16),
        ones_sc=np.ones((128, T), dtype=ml_dtypes.bfloat16),
    )
    for h in range(16):
        consts["sel16"][:, 16 * h + h] = 1.0
        consts["selbc"][h, 128 * h:128 * (h + 1)] = 1.0
    return consts


def build_program(taps=()):
    nc = bass.Bass("TRN2", target_bir_lowering=False)
    taps = set(taps)
    tap_list = []

    dx = nc.dram_tensor("x", [D, T], F32, kind="ExternalInput").ap()
    dwq = nc.dram_tensor("wq_t", [D, D], F32, kind="ExternalInput").ap()
    dwkv = nc.dram_tensor("wkv_t", [D, D], F32, kind="ExternalInput").ap()
    dwo = nc.dram_tensor("wo_t", [D, D], F32, kind="ExternalInput").ap()
    dgw = nc.dram_tensor("gw", [D, DI], F32, kind="ExternalInput").ap()
    duw = nc.dram_tensor("uw", [D, DI], F32, kind="ExternalInput").ap()
    ddw = nc.dram_tensor("dw", [DI, D], F32, kind="ExternalInput").ap()
    dgs = nc.dram_tensor("gscol", [128, NIC], F32, kind="ExternalInput").ap()
    dus = nc.dram_tensor("uscol", [128, NIC], F32, kind="ExternalInput").ap()
    dds = nc.dram_tensor("dscol", [128, 16], F32, kind="ExternalInput").ap()
    dmask = nc.dram_tensor("maskag", [128, 8], F32, kind="ExternalInput").ap()
    dcm = nc.dram_tensor("cm", [128, 128], BF16, kind="ExternalInput").ap()
    dsm = nc.dram_tensor("sm", [128, 128], BF16, kind="ExternalInput").ap()
    dsel = nc.dram_tensor("sel16", [128, 256], BF16, kind="ExternalInput").ap()
    dselbc = nc.dram_tensor("selbc", [16, 2048], BF16, kind="ExternalInput").ap()
    dnegc = nc.dram_tensor("negc16", [16, 128], BF16, kind="ExternalInput").ap()
    dposc = nc.dram_tensor("posc16", [16, 1], BF16, kind="ExternalInput").ap()
    dones1 = nc.dram_tensor("ones1x128", [1, 128], BF16, kind="ExternalInput").ap()
    dones_sc = nc.dram_tensor("ones_sc", [128, T], BF16, kind="ExternalInput").ap()
    dout = nc.dram_tensor("out", [D, T], F32, kind="ExternalOutput").ap()
    cc_in = nc.dram_tensor("cc_in", [128, 32], F32)
    cc_out = nc.dram_tensor("cc_out", [128 * 8, 32], F32)

    ctx = contextlib.ExitStack()
    sbuf = lambda nm, sh, dt: ctx.enter_context(nc.sbuf_tensor(nm, sh, dt))
    psum = lambda nm, sh: ctx.enter_context(nc.psum_tensor(nm, sh, F32))

    x_sb = sbuf("x_sb", [128, NKC * T], F32)
    z_sb = sbuf("z_sb", [128, 32 * T], BF16)
    pool = sbuf("pool", [128, 32 * T], BF16)
    cm_sb = sbuf("cm_sb", [128, 128], BF16)
    sm_sb = sbuf("sm_sb", [128, 128], BF16)
    sel_sb = sbuf("sel_sb", [128, 256], BF16)
    selbc_sb = sbuf("selbc_sb", [16, 2048], BF16)
    negc_sb = sbuf("negc_sb", [16, 128], BF16)
    posc_sb = sbuf("posc_sb", [16, 1], BF16)
    ones1_sb = sbuf("ones1_sb", [1, 128], BF16)
    mask_sb = sbuf("mask_sb", [128, 8], F32)
    gs_sb = sbuf("gs_sb", [128, NIC], F32)
    us_sb = sbuf("us_sb", [128, NIC], F32)
    ds_sb = sbuf("ds_sb", [128, 16], F32)
    rnk16 = sbuf("rnk16", [16, T], BF16)
    xb_t = sbuf("xb_t", [128, T], BF16)
    sq_t = sbuf("sq_t", [128, T], BF16)
    wmat = sbuf("wmat", [128, 16], BF16)
    bc2a = sbuf("bc2a", [128, T], BF16)
    sums_sb = sbuf("sums_sb", [128, 32], F32)
    carry_sb = sbuf("carry_sb", [128, 32], F32)
    nrows = z_sb[0:16, 8 * T:9 * T]
    nrows2 = z_sb[0:16, 9 * T:10 * T]
    rnq16 = pool[0:16, 12 * T:13 * T]
    rowbuf = pool[0:1, 16 * T:16 * T + 2 * T]
    poolf32 = pool.bitcast(F32)
    gath_sb = poolf32[:, 7680:7680 + 256]
    ones_sc_sb = pool[:, 11 * T:12 * T]

    ps_a = psum("ps_a", [128, T])
    ps_b = psum("ps_b", [128, T])
    ps_c = psum("ps_c", [128, T])
    ps_d = psum("ps_d", [128, T])
    PS = [ps_a, ps_b]

    P = Prog()

    def slot(i, half=None):
        if half is None:
            return pool[:, i * T:(i + 1) * T]
        return pool[:, i * T + half * TH: i * T + (half + 1) * TH]

    def zslot(i, half=None):
        if half is None:
            return z_sb[:, i * T:(i + 1) * T]
        return z_sb[:, i * T + half * TH: i * T + (half + 1) * TH]

    def xch(fc, half=None):
        if half is None:
            return x_sb[:, fc * T:(fc + 1) * T]
        return x_sb[:, fc * T + half * TH: fc * T + (half + 1) * TH]

    def tap(name, ap, shape, dt, deps):
        if name not in taps:
            return
        t = nc.dram_tensor("tap_" + name, shape, dt, kind="ExternalOutput").ap()
        tap_list.append(name)
        for d in deps:
            P.wait("sync", d)
        P.dma("sync", t, ap)

    ld_x = P.dma("sync", x_sb.rearrange("p (fc t) -> p fc t", t=T),
                 dx.rearrange("(fc p) t -> p fc t", p=128))
    for dst, src in ((cm_sb, dcm), (sm_sb, dsm), (sel_sb, dsel), (selbc_sb, dselbc),
                     (negc_sb, dnegc), (posc_sb, dposc), (ones1_sb, dones1),
                     (mask_sb, dmask),
                     (gs_sb, dgs), (us_sb, dus), (ds_sb, dds)):
        P.dma("sync", dst[:, :], src[:, :])
    ld_consts = P.token("sync")

    def wtile(buf, kc, hq):
        base = (16 + 8 * buf) * T + kc * TH + hq * 128
        return pool[:, base:base + 128]

    def stream_quarter(dw_, qn, buf, guard=None):
        if guard is not None:
            P.wait("gpsimd", guard)
        base = (16 + 8 * buf) * T
        dst = pool[:, base:base + 8 * T].rearrange("p (kc c) -> p kc c", c=TH)
        src = dw_.rearrange("(kc p) d -> p kc d", p=128)[:, :, TH * qn:TH * (qn + 1)]
        return P.dma("gpsimd", dst, src)

    P.wait("scalar", ld_x)
    P.wait("tensor", ld_consts)
    mm_prev = None
    for i in range(2 * NKC):
        fc, th = divmod(i, 2)
        cb = xb_t[:, (i % 2) * TH:((i % 2) + 1) * TH]
        if i >= 2:
            P.wait("scalar", mm_prev)
        ctk = P.op("scalar", lambda E, o=cb, s=xch(fc, th): E.copy(out=o, in_=s), inc=1)
        P.wait("tensor", ctk)
        mm_prev = P.op("tensor", lambda E, o=ps_c[0:16, th * TH:(th + 1) * TH],
                       l=sel_sb[:, 16 * fc:16 * fc + 16], r=cb,
                       s=(fc == 0), st=(fc == NKC - 1):
                       E.matmul(o, l, r, start=s, stop=st), inc=1)
    P.wait("vector", mm_prev)
    ev1 = P.op("vector", lambda E: E.tensor_copy(nrows, ps_c[0:16, :]), inc=1)
    P.wait("tensor", ev1)
    bcmm = None
    for th in range(2):
        bcmm = P.op("tensor", lambda E, o=ps_a[:, th * TH:(th + 1) * TH],
                    r=nrows[:, th * TH:(th + 1) * TH]:
                    E.matmul(o, negc_sb[:, :], r, start=True, stop=True), inc=(th == 1))
    bneg1 = zslot(31)
    P.wait("scalar", bcmm)
    bneg1_tok = P.op("scalar", lambda E: E.copy(out=bneg1, in_=ps_a[:, :]), inc=1)
    x1_tok = None
    x1_chunk = {}
    for fc in range(NKC):
        P.wait("vector", bneg1_tok)
        x1_chunk[fc] = P.op("vector", lambda E, o=slot(fc), a=xch(fc):
                            E.tensor_add(o, a, bneg1), inc=1)
        x1_tok = x1_chunk[fc]
    tap("x1", pool[:, 0:NKC * T], [128, NKC * T], BF16, [x1_tok])

    sqbufs = [xb_t, sq_t]

    def emit_norm(h, psn):
        sqb = sqbufs[h % 2]
        nmm = None
        for th in range(2):
            nmm = P.op("tensor", lambda E, o=psn[0:16, th * TH:(th + 1) * TH],
                       l=sel_sb[:, 16 * h:16 * h + 16], r=sqb[:, th * TH:(th + 1) * TH],
                       s=(h == 0), st=(h == NH - 1):
                       E.matmul(o, l, r, start=s, stop=st),
                       inc=(1 if (h == NH - 1 and th == 1) else 0))
        return nmm

    def proj_head(h, wbuf, psn, psz, first=False):
        hq = h % 4
        nmm = None
        if h > 0:
            P.wait("tensor", guards[(0 if psn is ps_d else NH) + h - 1])
            nmm = emit_norm(h - 1, psn)
        for th in range(2):
            for kc in range(NKC):
                if first and th == 0:
                    P.wait("tensor", x1_chunk[kc])
                last = kc == NKC - 1
                P.op("tensor", lambda E, o=psz[:, th * TH:(th + 1) * TH],
                     l=wtile(wbuf, kc, hq), r=slot(kc, th), s=(kc == 0), st=last:
                     E.matmul(o, l, r, start=s, stop=st), inc=1 if last else 0)
        mm_tok = P.token("tensor")
        P.wait("scalar", mm_tok)
        zdst = zslot(h + (0 if psn is ps_c else 16))
        zev = P.op("scalar", lambda E, o=zdst, i=psz[:, :]: E.copy(out=o, in_=i), inc=1)
        sqb = sqbufs[h % 2]
        if h >= 2:
            P.wait("scalar", nmm)
        sqev = P.op("scalar", lambda E, o=sqb[:, :], i=psz[:, :]:
                    E.activation(out=o, in_=i, func=AF.Square), inc=1)
        if h == NH - 1:
            P.wait("tensor", sqev)
            nmm = emit_norm(h, psn)
        return mm_tok, zev, sqev, nmm

    zf32 = z_sb.bitcast(F32)
    def finalize_norm(psn, scratch, dst):
        tok = P.op("vector", lambda E: E.reciprocal(scratch, psn[0:16, :]), inc=1)
        P.wait("scalar", tok)
        return P.op("scalar", lambda E: E.activation(out=dst, in_=scratch,
                                                     func=AF.Sqrt, scale=1.0 / 128.0), inc=1)

    qtr_done = {}
    pref = {("kv", 0): stream_quarter(dwkv, 0, 0), ("kv", 1): stream_quarter(dwkv, 1, 1)}
    guards = {}
    kv_nmm = None
    for h in range(NH):
        qn, buf = h // 4, (h // 4) % 2
        if h % 4 == 0:
            P.wait("tensor", pref[("kv", qn)])
        if h >= 2:
            P.wait("tensor", guards[h - 2])
        mm_tok, zev, sqev, kv_nmm = proj_head(h, buf, ps_d, PS[h % 2], first=(h == 0))
        guards[h] = sqev
        if h % 4 == 3:
            qtr_done[("kv", qn)] = mm_tok
            if qn < 3:
                pref[("kv", qn + 1)] = pref.get(("kv", qn + 1)) or \
                    stream_quarter(dwkv, qn + 1, (qn + 1) % 2,
                                   guard=qtr_done.get(("kv", qn - 1)))
    P.wait("vector", kv_nmm)
    rnk_tok = finalize_norm(ps_d, zf32[0:16, 0:T], rnk16[:, :])
    tap("rnk16", rnk16[:, :], [16, T], BF16, [rnk_tok])

    pref[("q", 0)] = stream_quarter(dwq, 0, 0, guard=qtr_done[("kv", 2)])
    pref[("q", 1)] = stream_quarter(dwq, 1, 1, guard=qtr_done[("kv", 3)])
    P.wait("tensor", rnk_tok)
    P.wait("tensor", kv_nmm)
    prev_bcev = None
    cst = None
    sums_ev = None
    cc_send = None
    cc_tok = None
    smm = None
    q_nmm = None
    def sums_head(hs):
        bcmm = None
        for th in range(2):
            bcmm = P.op("tensor", lambda E, o=ps_d[:, th * TH:(th + 1) * TH],
                        l=selbc_sb[:, 128 * hs:128 * (hs + 1)],
                        r=rnk16[:, th * TH:(th + 1) * TH]:
                        E.matmul(o, l, r, start=True, stop=True), inc=(th == 1))
        P.wait("scalar", bcmm)
        bcev = P.op("scalar", lambda E: E.copy(out=bc2a[:, :], in_=ps_d[:, :]), inc=1)
        P.wait("vector", bcev)
        cst = P.op("vector", lambda E, z=zslot(16 + hs), hh=hs: E.scalar_tensor_tensor(
            bc2a[:, :], z, 1.0, bc2a[:, :], ALU.mult, ALU.mult,
            accum_out=wmat[:, hh:hh + 1]), inc=1)
        return bcev, cst

    for h in range(NH):
        qn, buf = h // 4, (h // 4) % 2
        if h < 8:
            for hs in (2 * h, 2 * h + 1):
                if prev_bcev is not None:
                    P.wait("tensor", prev_bcev)
                prev_bcev, cst = sums_head(hs)
        if h == 8:
            tap("wmat", wmat[:, :], [128, 16], BF16, [cst])
            P.wait("tensor", cst)
            P.op("tensor", lambda E: E.matmul(ps_d[:, 0:16], cm_sb[:, :], wmat[:, :],
                                              start=True, stop=True), inc=0)
            smm = P.op("tensor", lambda E: E.matmul(ps_d[:, 16:32], sm_sb[:, :], wmat[:, :],
                                                    start=True, stop=True), inc=1)
            P.wait("vector", smm)
            sums_ev = P.op("vector", lambda E: E.tensor_copy(sums_sb[:, :], ps_d[:, 0:32]), inc=1)
            P.wait("sync", sums_ev)
            cc_send = P.dma("sync", cc_in.ap()[:, :], sums_sb[:, :])
            P.wait("gpsimd", cc_send)
            cc_tok = P.collective(lambda E: E.collective_compute(
                "AllGather", ALU.bypass, replica_groups=[list(range(8))],
                ins=[cc_in.ap().opt()], outs=[cc_out.ap().opt()]))
        if h % 4 == 0:
            P.wait("tensor", pref[("q", qn)])
        P.wait("tensor", guards[NH - 2 + h] if h < 2 else guards[NH + h - 2])
        mm_tok, zev, sqev, q_nmm = proj_head(h, buf, ps_c, PS[h % 2])
        guards[NH + h] = sqev
        if h % 4 == 3:
            qtr_done[("q", qn)] = mm_tok
            if qn < 3:
                pref[("q", qn + 1)] = pref.get(("q", qn + 1)) or \
                    stream_quarter(dwq, qn + 1, (qn + 1) % 2,
                                   guard=qtr_done.get(("q", qn - 1)))
    P.wait("vector", q_nmm)
    rnq_tok = finalize_norm(ps_c, poolf32[0:16, 6656:6656 + T], rnq16)
    tap("rnq16", rnq16, [16, T], BF16, [rnq_tok])
    tap("zq0", zslot(0), [128, T], BF16, [guards[NH]])
    tap("zkv0", zslot(16), [128, T], BF16, [guards[0]])
    tap("sums", sums_sb[:, :], [128, 32], F32, [sums_ev])
    P.wait("sync", qtr_done[("q", 3)])
    ones_ld = P.dma("sync", ones_sc_sb, dones_sc[:, :])

    P.wait("sync", cc_tok)
    P.wait("sync", qtr_done[("q", 3)])
    gl = None
    for r in range(8):
        gl = P.dma("sync", gath_sb[:, 32 * r:32 * (r + 1)],
                   cc_out.ap()[128 * r:128 * (r + 1), :])
    P.wait("vector", gl)
    P.op("vector", lambda E: E.memset(carry_sb[:, :], 0.0), inc=0)
    carry_tok = None
    for r in range(8):
        carry_tok = P.op("vector", lambda E, rr=r: E.scalar_tensor_tensor(
            carry_sb[:, :], gath_sb[:, 32 * rr:32 * (rr + 1)], mask_sb[:, rr:rr + 1],
            carry_sb[:, :], ALU.mult, ALU.add), inc=1)
    tap("carry", carry_sb[:, :], [128, 32], F32, [carry_tok])

    t_cr, t_ci = slot(2), slot(3)
    t_qr, t_qi = slot(4), slot(5)
    t_vr, t_vi = slot(6), slot(7)
    t_sc, t_sc2 = slot(8), slot(9)
    zs2 = [slot(13), slot(10)]
    pref[("o", 0)] = stream_quarter(dwo, 0, 0, guard=qtr_done[("q", 2)])
    pref[("o", 1)] = stream_quarter(dwo, 1, 1, guard=qtr_done[("q", 3)])
    flush1 = {}
    prev_tkv = None
    prev_tq = None
    prev_vev = None
    prev_q1 = None
    prev_s2 = None
    wo1_dc = 0
    wo1_flush = None
    wo1_mm = None

    def wo_half1_step(vev_tok, q2_tok):
        nonlocal wo1_dc, wo1_flush, wo1_mm
        if wo1_dc >= 16:
            return
        dc = wo1_dc
        qn, dcq = dc // 4, dc % 4
        P.wait("tensor", pref[("o", qn)])
        if wo1_flush is not None:
            P.wait("tensor", wo1_flush)
        else:
            P.wait("tensor", q2_tok)
        mmv = None
        for th in range(2):
            for hh in range(8):
                last = hh == 7
                mmv = P.op("tensor", lambda E, o=ps_d[:, th * TH:(th + 1) * TH],
                           l=wtile(qn % 2, hh, dcq), r=zslot(16 + hh, th),
                           s=(hh == 0), st=last:
                           E.matmul(o, l, r, start=s, stop=st),
                           inc=(1 if (last and th == 1) else 0))
        wo1_mm = mmv
        P.wait("vector", mmv)
        wo1_flush = P.op("vector", lambda E, o=xch(dc), i=ps_d[:, :]:
                         E.tensor_add(o, o, i), inc=1)
        flush1[dc] = wo1_flush
        wo1_dc += 1
        if wo1_dc == 4:
            pref[("o", 2)] = stream_quarter(dwo, 2, 0, guard=wo1_mm)
        if wo1_dc == 8:
            pref[("o", 3)] = stream_quarter(dwo, 3, 1, guard=wo1_mm)
        if wo1_dc == 12:
            pref[("o2", 0)] = stream_quarter(dwo, 0, 0, guard=wo1_mm)
        if wo1_dc == 16:
            pref[("o2", 1)] = stream_quarter(dwo, 1, 1, guard=wo1_mm)

    for h in range(NH):
        P.wait("tensor", rnq_tok)
        P.wait("tensor", carry_tok)
        zs = zs2[h % 2]
        if h == 0:
            P.wait("tensor", qtr_done[("q", 3)])
        if prev_vev is not None:
            P.wait("tensor", prev_vev)
        mm1 = mm2 = None
        for th in range(2):
            mm1 = P.op("tensor", lambda E, o=ps_a[:, th * TH:(th + 1) * TH],
                       l=selbc_sb[:, 128 * h:128 * (h + 1)],
                       r=rnk16[:, th * TH:(th + 1) * TH]:
                       E.matmul(o, l, r, start=True, stop=True), inc=(th == 1))
        if prev_tq is not None:
            P.wait("tensor", prev_tq)
        for th in range(2):
            mm2 = P.op("tensor", lambda E, o=ps_b[:, th * TH:(th + 1) * TH],
                       l=selbc_sb[:, 128 * h:128 * (h + 1)],
                       r=rnq16[:, th * TH:(th + 1) * TH]:
                       E.matmul(o, l, r, start=True, stop=True), inc=(th == 1))
        P.wait("vector", mm1)
        tkv = P.op("vector", lambda E, z=zslot(16 + h), o=zs:
                   E.tensor_tensor(o, z, ps_a[:, :], ALU.mult), inc=1)
        P.wait("tensor", tkv)
        if prev_q1 is not None:
            P.wait("tensor", prev_q1)
        f1 = f2 = None
        for th in range(2):
            f1 = P.op("tensor", lambda E, o=ps_c[:, th * TH:(th + 1) * TH], rr=(th * TH, (th + 1) * TH), z=zs:
                      E.matmul(o, cm_sb[:, :], z[:, rr[0]:rr[1]], start=True, stop=True), inc=(th == 1))
        if h >= 8:
            if wo1_flush is not None:
                P.wait("tensor", wo1_flush)
        for th in range(2):
            f2 = P.op("tensor", lambda E, o=ps_d[:, th * TH:(th + 1) * TH], rr=(th * TH, (th + 1) * TH), z=zs:
                      E.matmul(o, sm_sb[:, :], z[:, rr[0]:rr[1]], start=True, stop=True), inc=(th == 1))
        if h == 0:
            P.wait("vector", ones_ld)
        P.wait("vector", f1)
        s1 = P.op("vector", lambda E, ic=carry_sb[:, h:h + 1]:
                  E.tensor_tensor_scan(t_cr, ones_sc_sb, ps_c[:, :], ic,
                                       ALU.mult, ALU.add), inc=1)
        P.wait("vector", f2)
        s2 = P.op("vector", lambda E, ic=carry_sb[:, 16 + h:16 + h + 1]:
                  E.tensor_tensor_scan(t_ci, ones_sc_sb, ps_d[:, :], ic,
                                       ALU.mult, ALU.add), inc=1)
        zsq = zs2[(h + 1) % 2]
        P.wait("vector", mm2)
        tq = P.op("vector", lambda E, z=zslot(h), o=zsq:
                  E.tensor_tensor(o, z, ps_b[:, :], ALU.mult), inc=1)
        P.wait("tensor", tq)
        P.wait("tensor", s1)
        g1 = g2 = None
        for th in range(2):
            g1 = P.op("tensor", lambda E, o=ps_c[:, th * TH:(th + 1) * TH], rr=(th * TH, (th + 1) * TH), z=zsq:
                      E.matmul(o, cm_sb[:, :], z[:, rr[0]:rr[1]], start=True, stop=True), inc=(th == 1))
        P.wait("tensor", s2)
        for th in range(2):
            g2 = P.op("tensor", lambda E, o=ps_d[:, th * TH:(th + 1) * TH], rr=(th * TH, (th + 1) * TH), z=zsq:
                      E.matmul(o, sm_sb[:, :], z[:, rr[0]:rr[1]], start=True, stop=True), inc=(th == 1))
        P.wait("scalar", g1)
        q1 = P.op("scalar", lambda E: E.copy(out=t_qr, in_=ps_c[:, :]), inc=1)
        P.wait("scalar", g2)
        q2 = P.op("scalar", lambda E: E.copy(out=t_qi, in_=ps_d[:, :]), inc=1)
        P.wait("gpsimd", q2)
        P.wait("gpsimd", s2)
        gp1 = P.op("gpsimd", lambda E: E.tensor_tensor(t_sc, t_qi, t_ci, ALU.mult), inc=1)
        gp2 = P.op("gpsimd", lambda E: E.tensor_tensor(t_vi, t_qi, t_cr, ALU.mult), inc=1)
        P.wait("vector", q1)
        P.op("vector", lambda E: E.tensor_mul(t_vr, t_qr, t_cr), inc=0)
        P.wait("vector", gp1)
        P.op("vector", lambda E: E.tensor_sub(t_vr, t_vr, t_sc), inc=0)
        P.op("vector", lambda E: E.tensor_mul(t_sc2, t_qr, t_ci), inc=0)
        P.wait("vector", gp2)
        prod = P.op("vector", lambda E: E.tensor_add(t_vi, t_sc2, t_vi), inc=1)
        P.wait("tensor", prod)
        vm = None
        for th in range(2):
            P.op("tensor", lambda E, o=ps_a[:, th * TH:(th + 1) * TH], rr=(th * TH, (th + 1) * TH):
                 E.matmul(o, cm_sb[:, :], t_vr[:, rr[0]:rr[1]], start=True, stop=False), inc=0)
            vm = P.op("tensor", lambda E, o=ps_a[:, th * TH:(th + 1) * TH], rr=(th * TH, (th + 1) * TH):
                      E.matmul(o, sm_sb[:, :], t_vi[:, rr[0]:rr[1]], start=False, stop=True), inc=(th == 1))
        P.wait("scalar", vm)
        vev = P.op("scalar", lambda E, o=zslot(16 + h): E.activation(
            out=o, in_=ps_a[:, :], func=AF.Copy, scale=1.0 / 128.0), inc=1)
        prev_tkv, prev_tq, prev_vev, prev_q1, prev_s2 = tkv, tq, vev, q1, s2
        if h >= 8:
            wo_half1_step(vev, q2)
            wo_half1_step(vev, q2)
        if h == 0:
            tap("kvc0", t_cr, [128, T], BF16, [s1])
            tap("qfn0", t_qr, [128, T], BF16, [q1])
            tap("v0", zslot(16), [128, T], BF16, [vev])

    flush = {}
    nfl = 0
    last_flush = None
    mm_m = mm_s = None
    h2mm = {}
    lnc2 = {}
    lncb = [xb_t, bc2a]
    lnsq = [sq_t, pool[:, 13 * T:14 * T]]
    for qn in range(4):
        buf = qn % 2
        if qn == 2:
            pref[("o2", 2)] = stream_quarter(dwo, 2, 0, guard=h2mm[0])
        if qn == 3:
            pref[("o2", 3)] = stream_quarter(dwo, 3, 1, guard=h2mm[1])
        P.wait("tensor", pref[("o2", qn)])
        for dcq in range(4):
            dc = 4 * qn + dcq
            psd = PS[dc % 2]
            if nfl >= 2:
                P.wait("tensor", flush[nfl - 2])
            elif nfl == 0:
                P.wait("tensor", prev_vev)
            mmv = None
            for th in range(2):
                for hh in range(8, NH):
                    last = hh == NH - 1
                    mmv = P.op("tensor", lambda E, o=psd[:, th * TH:(th + 1) * TH],
                               l=wtile(buf, hh, dcq), r=zslot(16 + hh, th),
                               s=(hh == 8), st=last:
                               E.matmul(o, l, r, start=s, stop=st),
                               inc=(1 if (last and th == 1) else 0))
            P.wait("vector", mmv)
            P.wait("vector", flush1[dc])
            ft = P.op("vector", lambda E, o=xch(dc), i=psd[:, :]:
                      E.tensor_add(o, o, i), inc=1)
            flush[nfl] = ft
            nfl += 1
            last_flush = ft
            h2mm[qn] = mmv
            P.wait("scalar", ft)
            c1 = P.op("scalar", lambda E, o=lncb[dc % 2][:, :], s=xch(dc): E.copy(out=o, in_=s), inc=1)
            c2 = P.op("scalar", lambda E, o=lnsq[dc % 2][:, :], s=xch(dc):
                      E.activation(out=o, in_=s, func=AF.Square), inc=1)
            lnc2[dc] = c2

            def emit_stageA(d):
                nonlocal mm_m, mm_s
                P.wait("tensor", lnc2[d])
                for th in range(2):
                    mm_m = P.op("tensor", lambda E, o=ps_c[0:16, th * TH:(th + 1) * TH],
                                l=sel_sb[:, 16 * d:16 * d + 16], r=lncb[d % 2][:, th * TH:(th + 1) * TH],
                                s=(d == 0), st=(d == NKC - 1):
                                E.matmul(o, l, r, start=s, stop=st), inc=0)
                    mm_s = P.op("tensor", lambda E, o=ps_d[0:16, th * TH:(th + 1) * TH],
                                l=sel_sb[:, 16 * d:16 * d + 16], r=lnsq[d % 2][:, th * TH:(th + 1) * TH],
                                s=(d == 0), st=(d == NKC - 1):
                                E.matmul(o, l, r, start=s, stop=st), inc=(1 if th == 1 else 0))
            if dc > 0:
                emit_stageA(dc - 1)
            if dc == 15:
                emit_stageA(15)
    tap("xattn", x_sb[:, :], [128, NKC * T], F32, [last_flush])

    P.wait("vector", mm_s)
    P.op("vector", lambda E: E.tensor_copy(nrows, ps_c[0:16, :]), inc=0)
    e_s = P.op("vector", lambda E: E.tensor_copy(nrows2, ps_d[0:16, :]), inc=1)
    P.wait("tensor", e_s)
    r_m = r_s = None
    for th in range(2):
        r_m = P.op("tensor", lambda E, o=ps_a[0:1, th * TH:(th + 1) * TH],
                   r=nrows[:, th * TH:(th + 1) * TH]:
                   E.matmul(o, posc_sb[:, :], r, start=True, stop=True), inc=(th == 1))
        r_s = P.op("tensor", lambda E, o=ps_b[0:1, th * TH:(th + 1) * TH],
                   r=nrows2[:, th * TH:(th + 1) * TH]:
                   E.matmul(o, posc_sb[:, :], r, start=True, stop=True), inc=(th == 1))
    P.wait("vector", r_s)
    rowm = poolf32[0:1, 9216:9216 + T]
    rowv = poolf32[0:1, 10240:10240 + T]
    rowr = poolf32[0:1, 11264:11264 + T]
    P.op("vector", lambda E: E.tensor_copy(rowm, ps_a[0:1, :]), inc=0)
    P.op("vector", lambda E: E.tensor_mul(rowv, ps_a[0:1, :], rowm), inc=0)
    P.op("vector", lambda E: E.tensor_sub(rowv, ps_b[0:1, :], rowv), inc=0)
    P.op("vector", lambda E: E.tensor_scalar_add(rowv, rowv, 1e-5), inc=0)
    rec = P.op("vector", lambda E: E.reciprocal(rowr, rowv), inc=1)
    nm2 = P.op("vector", lambda E: E.tensor_scalar_mul(rowbuf[0:1, 0:T], rowm, -1.0), inc=1)
    P.wait("scalar", rec)
    r2t = P.op("scalar", lambda E: E.activation(out=rowbuf[0:1, T:2 * T],
                                                in_=rowr, func=AF.Sqrt), inc=1)
    P.wait("tensor", r2t)
    P.wait("tensor", nm2)
    b_m = b_r = None
    for th in range(2):
        b_m = P.op("tensor", lambda E, o=ps_c[:, th * TH:(th + 1) * TH],
                   r=rowbuf[0:1, th * TH:(th + 1) * TH]:
                   E.matmul(o, ones1_sb[:, :], r, start=True, stop=True), inc=(th == 1))
        b_r = P.op("tensor", lambda E, o=ps_d[:, th * TH:(th + 1) * TH],
                   r=rowbuf[0:1, T + th * TH:T + (th + 1) * TH]:
                   E.matmul(o, ones1_sb[:, :], r, start=True, stop=True), inc=(th == 1))
    bneg2, br2 = zslot(31), zslot(30)
    P.wait("scalar", b_m)
    P.op("scalar", lambda E: E.copy(out=bneg2, in_=ps_c[:, :]), inc=0)
    eb2 = P.op("scalar", lambda E: E.copy(out=br2, in_=ps_d[:, :]), inc=1)
    x2_tok = None
    x2_chunk = {}
    for fc in range(NKC):
        P.wait("vector", eb2)
        P.op("vector", lambda E, o=slot(fc), a=xch(fc): E.tensor_add(o, a, bneg2), inc=0)
        x2_chunk[fc] = P.op("vector", lambda E, o=slot(fc): E.tensor_mul(o, o, br2), inc=1)
        x2_tok = x2_chunk[fc]
    tap("x2", pool[:, 0:NKC * T], [128, NKC * T], BF16, [x2_tok])

    z8 = z_sb.bitcast(FP8)

    def dgrp(i):
        return z_sb[:, 2048 * i:2048 * (i + 1)]

    def gbuf(b):
        return z_sb[:, 16384 + 2048 * b:16384 + 2048 * (b + 1)]

    def ubuf(b):
        return z_sb[:, 20480 + 2048 * b:20480 + 2048 * (b + 1)]

    def g8(b):
        return z8[:, 49152 + 2048 * b:49152 + 2048 * (b + 1)]

    def u8(b):
        return z8[:, 53248 + 2048 * b:53248 + 2048 * (b + 1)]

    def d8(b):
        return z8[:, 57344 + 2048 * b:57344 + 2048 * (b + 1)]

    gsrc = dgw.rearrange("(kc p) i -> p kc i", p=128)
    usrc = duw.rearrange("(kc p) i -> p kc i", p=128)

    def load8(ic, b):
        P.dma("gpsimd", g8(b).rearrange("p (kc c) -> p kc c", c=128),
              gsrc[:, :, 128 * ic:128 * (ic + 1)])
        P.dma("gpsimd", u8(b).rearrange("p (kc c) -> p kc c", c=128),
              usrc[:, :, 128 * ic:128 * (ic + 1)])
        return P.dma("gpsimd", d8(b), ddw[128 * ic:128 * (ic + 1), :])

    def upcast_gu(ic):
        b = ic % 2
        P.wait("gpsimd", ld8[ic])
        if ic >= 2:
            P.wait("gpsimd", fuse_tok[ic - 2])
        P.dma("gpsimd", gbuf(b), g8(b))
        return P.dma("gpsimd", ubuf(b), u8(b))

    def upcast_d(ic):
        b, icg = ic % 2, ic % GSZ
        g = ic // GSZ
        if g > 0 and icg == 0:
            P.wait("gpsimd", dmm_done[g - 1])
        return P.dma("gpsimd", dgrp(icg), d8(b))

    P.wait("gpsimd", P.token("tensor"))
    ld8 = {0: load8(0, 0), 1: load8(1, 1)}
    upc = {}
    upcd = {}
    fuse_tok = {}
    dmm_done = {}
    flf = {}
    silu_tok = {}
    nffl = 0
    last_ffn = None
    st_out = None
    upc[0] = upcast_gu(0)
    upcd[0] = upcast_d(0)
    GPS = [ps_a, ps_c]
    UPS = [ps_b, ps_d]
    for g in range(NG):
        for icg in range(GSZ):
            ic = g * GSZ + icg
            b = ic % 2
            if ic > 0:
                upcd[ic] = upcast_d(ic)
            if ic + 2 < NIC:
                ld8[ic + 2] = load8(ic + 2, b)
            if ic + 1 < NIC:
                upc[ic + 1] = upcast_gu(ic + 1)
            P.wait("tensor", upc[ic])
            if ic == 0:
                P.wait("tensor", eb2)
            if ic >= 2:
                P.wait("tensor", fuse_tok[ic - 2])
            if icg == 0 and g > 0:
                P.wait("tensor", flf[nffl - 1])
            mg = mu = None
            psg, psu = GPS[ic % 2], UPS[ic % 2]
            for th in range(2):
                for kc in range(NKC):
                    if ic == 0 and th == 0:
                        P.wait("tensor", x2_chunk[kc])
                    last = kc == NKC - 1
                    mg = P.op("tensor", lambda E, o=psg[:, th * TH:(th + 1) * TH],
                              l=gbuf(b)[:, 128 * kc:128 * (kc + 1)], r=slot(kc, th),
                              s=(kc == 0), st=last:
                              E.matmul(o, l, r, start=s, stop=st),
                              inc=(1 if (last and th == 1) else 0))
            for th in range(2):
                for kc in range(NKC):
                    last = kc == NKC - 1
                    mu = P.op("tensor", lambda E, o=psu[:, th * TH:(th + 1) * TH],
                              l=ubuf(b)[:, 128 * kc:128 * (kc + 1)], r=slot(kc, th),
                              s=(kc == 0), st=last:
                              E.matmul(o, l, r, start=s, stop=st),
                              inc=(1 if (last and th == 1) else 0))
            gact = [xb_t[:, :], zslot(30)][ic % 2]
            uact = [sq_t[:, :], zslot(31)][ic % 2]
            P.wait("scalar", mg)
            sg = P.op("scalar", lambda E, i=ic, p=psg, o=gact: E.activation(
                out=o, in_=p[:, :], func=AF.Silu, scale=gs_sb[:, i:i + 1]), inc=1)
            silu_tok[ic] = sg
            P.wait("scalar", mu)
            su = P.op("scalar", lambda E, i=ic, p=psu, o=uact: E.activation(
                out=o, in_=p[:, :], func=AF.Copy, scale=us_sb[:, i:i + 1]), inc=1)
            P.wait("vector", su)
            fuse_tok[ic] = P.op("vector", lambda E, o=slot(16 + icg), a=gact, u=uact:
                                E.tensor_mul(o, a, u), inc=1)
        dlast = None
        for dc in range(16):
            psd = [ps_c, ps_d][dc % 2]
            if nffl >= 2:
                P.wait("tensor", flf[nffl - 2])
            if dc < 2:
                P.wait("tensor", silu_tok[g * GSZ + GSZ - 1])
                P.wait("tensor", fuse_tok[g * GSZ + GSZ - 1])
                P.wait("tensor", upc[g * GSZ + GSZ - 1])
            mmv = None
            for th in range(2):
                for icg in range(GSZ):
                    last = icg == GSZ - 1
                    mmv = P.op("tensor", lambda E, o=psd[:, th * TH:(th + 1) * TH],
                               l=dgrp(icg)[:, 128 * dc:128 * (dc + 1)],
                               r=slot(16 + icg, th), s=(icg == 0), st=last:
                               E.matmul(o, l, r, start=s, stop=st),
                               inc=(1 if (last and th == 1) else 0))
            dlast = mmv
            P.wait("vector", mmv)
            ft = P.op("vector", lambda E, o=xch(dc), i=psd[:, :], d=dc:
                      E.scalar_tensor_tensor(o, i, ds_sb[:, d:d + 1], o,
                                             ALU.mult, ALU.add), inc=1)
            flf[nffl] = ft
            nffl += 1
            last_ffn = ft
            if g == NG - 1:
                P.wait("sync", ft)
                st_out = P.dma("sync", dout.rearrange("(fc p) t -> p fc t", p=128)[:, dc, :],
                               xch(dc))
        dmm_done[g] = dlast
    P.wait("sync", st_out)

    with contextlib.ExitStack() as es:
        sems = {e: es.enter_context(nc.semaphore("sem_" + e)) for e in ENGINES}
        cc_sem = es.enter_context(nc.semaphore("sem_cc"))
        with nc.Block() as block:
            block.sync(lambda E: P.replay("sync", E, sems, cc_sem))
            block.tensor(lambda E: P.replay("tensor", E, sems, cc_sem))
            block.vector(lambda E: P.replay("vector", E, sems, cc_sem))
            block.scalar(lambda E: P.replay("scalar", E, sems, cc_sem))
            block.gpsimd(lambda E: P.replay("gpsimd", E, sems, cc_sem))
    ctx.close()
    return nc, tap_list


_BUILD_CACHE = {}


def _get_program(taps=()):
    key = tuple(sorted(taps))
    if key not in _BUILD_CACHE:
        _BUILD_CACHE[key] = build_program(taps)
    return _BUILD_CACHE[key]


def make_in_maps(L0_attn_x, L0_attn_W_q, L0_attn_W_kv, L0_attn_W_o,
                 L0_ffn_gate_w, L0_ffn_up_w, L0_ffn_down_w,
                 L0_ffn_gate_scale, L0_ffn_up_scale, L0_ffn_down_scale):
    x = np.asarray(L0_attn_x, dtype=np.float32)
    wq_t = np.ascontiguousarray(np.asarray(L0_attn_W_q, dtype=np.float32).T)
    wkv_t = np.ascontiguousarray(np.asarray(L0_attn_W_kv, dtype=np.float32).T)
    wo_t = np.ascontiguousarray(np.asarray(L0_attn_W_o, dtype=np.float32).T)
    gw = np.ascontiguousarray(np.asarray(L0_ffn_gate_w, dtype=np.float32))
    uw = np.ascontiguousarray(np.asarray(L0_ffn_up_w, dtype=np.float32))
    dw = np.ascontiguousarray(np.asarray(L0_ffn_down_w, dtype=np.float32))
    gscol = np.ascontiguousarray(np.asarray(L0_ffn_gate_scale, np.float32).reshape(NIC, 128).T)
    uscol = np.ascontiguousarray(np.asarray(L0_ffn_up_scale, np.float32).reshape(NIC, 128).T)
    dscol = np.ascontiguousarray(np.asarray(L0_ffn_down_scale, np.float32).reshape(16, 128).T)
    consts = host_consts()
    in_maps = []
    for c in range(8):
        b, hh = divmod(c, 2)
        xs = np.ascontiguousarray(x[b, T * hh:T * (hh + 1), :].T)
        mask = np.zeros((128, 8), dtype=np.float32)
        if hh == 1:
            mask[:, c - 1] = 1.0
        m = dict(x=xs, wq_t=wq_t, wkv_t=wkv_t, wo_t=wo_t, gw=gw, uw=uw, dw=dw,
                 gscol=gscol, uscol=uscol, dscol=dscol, maskag=mask, **consts)
        in_maps.append(m)
    return in_maps


def kernel(L0_attn_x, L0_attn_W_q, L0_attn_W_kv, L0_attn_W_o,
           L0_ffn_gate_w, L0_ffn_up_w, L0_ffn_down_w,
           L0_ffn_gate_scale, L0_ffn_up_scale, L0_ffn_down_scale):
    nc, _ = _get_program()
    in_maps = make_in_maps(L0_attn_x, L0_attn_W_q, L0_attn_W_kv, L0_attn_W_o,
                           L0_ffn_gate_w, L0_ffn_up_w, L0_ffn_down_w,
                           L0_ffn_gate_scale, L0_ffn_up_scale, L0_ffn_down_scale)
    res = run_bass_kernel_spmd(nc, in_maps, core_ids=list(range(8)))
    out = np.empty((4, 2048, 2048), dtype=np.float32)
    for c in range(8):
        b, hh = divmod(c, 2)
        out[b, T * hh:T * (hh + 1), :] = np.asarray(res.results[c]["out"]).T
    return out
